# version 13 (final)
# baseline (speedup 1.0000x reference)
import numpy as np
from contextlib import ExitStack
import concourse.bass as bass
import concourse.mybir as mybir
from concourse.bass_utils import run_bass_kernel_spmd

F32 = mybir.dt.float32
BF16 = mybir.dt.bfloat16
AF = mybir.ActivationFunctionType
ALU = mybir.AluOpType
AX = mybir.AxisListType.X

D = 1024
PHASE = 99
S0 = 26.0
E2S0 = float(np.exp(2 * S0))
EPS = 1e-5


class P:
    ENGINES = ("sync", "gpsimd", "scalar", "vector", "tensor")
    DMA_SEM = {"sync": "Yd", "gpsimd": "Gd", "scalar": "Sd"}
    PROG_SEM = {"sync": "Y", "gpsimd": "G", "scalar": "S", "vector": "V", "tensor": "T"}

    def __init__(self, nc):
        self.nc = nc
        self.ev = {e: [] for e in self.ENGINES}
        self.semh = {}
        self.semc = {}

    def _sem(self, name):
        if name not in self.semh:
            self.semh[name] = self.nc.alloc_semaphore(name)
            self.semc[name] = 0
        return name

    def val(self, name):
        return self.semc.get(name, 0)

    def emit(self, eng, fn, wait=(), dma=False):
        if dma:
            sem = dma if isinstance(dma, str) else self.DMA_SEM[eng]
            self._sem(sem)
            self.semc[sem] += 16
            v = self.semc[sem]
            wait = list(wait) + [(sem, v - 16)]
        else:
            sem = self.PROG_SEM[eng]
            self._sem(sem)
            prev = self.semc[sem]
            self.semc[sem] += 1
            v = self.semc[sem]
            wait = list(wait) + [(sem, prev)]
        w = {}
        for s, x in wait:
            if x and x > 0:
                self._sem(s)
                w[s] = max(w.get(s, 0), x)
        self.ev[eng].append((fn, tuple(w.items()), sem, 16 if dma else 1))
        return v

    def build(self, name=None):
        with self.nc.Block(name) as block:
            for eng in self.ENGINES:
                evs = self.ev[eng]
                if not evs:
                    continue

                def mk(evs):
                    def body(e):
                        for fn, waits, sem, by in evs:
                            for s, v in waits:
                                e.wait_ge(self.semh[s], v)
                            ins = fn(e)
                            assert ins is not None, "event returned no instruction"
                            ins.then_inc(self.semh[sem], by)
                    return body

                getattr(block, eng)(mk(evs))
        self.ev = {e: [] for e in self.ENGINES}

    def reg_floats(self, es, values, tag="cstf"):
        vals = [v for v in dict.fromkeys(values) if v not in getattr(self, "_regd", set())]
        if not hasattr(self, "_regd"):
            self._regd = set()
        if not vals:
            return 0
        es = getattr(self, "const_es", None) or es
        buf = es.enter_context(self.nc.sbuf_tensor(tag, [128, len(vals)], F32))

        def f(g):
            ins = None
            for i, v in enumerate(vals):
                ins = g.memset(buf[:, i:i + 1], float(v))
            return ins

        ev = self.emit("gpsimd", f)
        for i, v in enumerate(vals):
            self.nc.const_aps.register(F32, float(v), buf[:, i:i + 1])
            self._regd.add(v)
        return ev

    def snap(self):
        return [(s, self.val(s)) for s, v in self.semc.items() if v > 0]

    def barrier_all(self):
        w = self.snap()
        for eng in self.ENGINES:
            self.emit(eng, lambda e: e.nop(), wait=w)


def emit_ln(p, src_tiles, dst_tiles, stats, col0, sqscr, wait_src):
    nt = len(src_tiles)
    sl = lambda k: stats[:, col0 + k * nt: col0 + (k + 1) * nt]
    s1, s2, mu, rs, nb = (sl(k) for k in range(5))
    for i, t in enumerate(src_tiles):
        p.emit("vector", lambda v, i=i, t=t: v.reduce_sum(s1[:, i:i + 1], t, AX),
               wait=wait_src)
        p.emit("scalar", lambda s, i=i, t=t: s.activation(
            out=sqscr[:], in_=t, func=AF.Square, accum_out=s2[:, i:i + 1]),
            wait=wait_src)
    vv, sv = p.val("V"), p.val("S")
    p.emit("vector", lambda v: v.tensor_scalar_mul(mu[:], s1[:], 1.0 / D), wait=[("V", vv)])
    p.emit("vector", lambda v: v.tensor_scalar_mul(rs[:], s2[:], 1.0 / D), wait=[("S", sv)])
    p.emit("vector", lambda v: v.tensor_mul(nb[:], mu[:], mu[:]))
    p.emit("vector", lambda v: v.tensor_tensor(rs[:], rs[:], nb[:], ALU.subtract))
    vv = p.val("V")
    p.emit("scalar", lambda s: s.activation(out=rs[:], in_=rs[:], func=AF.Ln, bias=EPS),
           wait=[("V", vv)])
    p.emit("scalar", lambda s: s.activation(out=rs[:], in_=rs[:], func=AF.Exp, scale=-0.5))
    sv = p.val("S")
    p.emit("vector", lambda v: v.scalar_tensor_tensor(nb[:], mu[:], -1.0, rs[:],
                                                      ALU.mult, ALU.mult),
           wait=[("S", sv)])
    vv = p.val("V")
    for i, (srct, dstt) in enumerate(zip(src_tiles, dst_tiles)):
        p.emit("scalar", lambda s, i=i, srct=srct, dstt=dstt: s.activation(
            out=dstt, in_=srct, func=AF.Identity,
            scale=rs[:, i:i + 1], bias=nb[:, i:i + 1]),
            wait=[("V", vv)])
    return p.val("S")



def build_stageA(p, es, T, xin, x1o):
    nc = p.nc
    wts = {k: nc.dram_tensor(k, [D, D], F32, kind="ExternalInput").ap()
           for k in ("L0_attn_Wx", "L0_attn_Ww", "L0_attn_Wk", "L0_attn_Wv",
                     "L0_attn_Wr", "L0_attn_Wo")}
    vecs = {k: nc.dram_tensor(k, [D], F32, kind="ExternalInput").ap()
            for k in ("L0_attn_ln_g", "L0_attn_ln_b", "L0_attn_bw")}
    NB = T // 512
    sb = lambda name, shape, dt=F32: es.enter_context(nc.sbuf_tensor(name, shape, dt))

    identT = sb("identT", [128, 128], BF16)
    U64 = sb("U64", [128, 128], F32)
    Bsel = sb("Bsel", [128, 128], F32)
    cind = sb("cind", [128, 2], F32)
    maskE = sb("maskE", [128, 128], F32)
    ones1 = sb("ones1", [1, 128], F32)

    def c_memsets(g):
        g.memset(identT[:], 0.0)
        g.memset(U64[:], 0.0)
        g.memset(Bsel[:], 0.0)
        g.memset(cind[:], 0.0)
        g.memset(maskE[:], 0.0)
        return g.memset(ones1[:], 1.0)

    def c_fills(g):
        g.affine_select(out=identT[:], in_=identT[:], compare_op=ALU.not_equal,
                        fill=1.0, base=0, pattern=[[-1, 128]], channel_multiplier=1)
        for c in range(2):
            blk = U64[64 * c:64 * c + 64, 64 * c:64 * c + 64]
            g.affine_select(out=blk, in_=blk, compare_op=ALU.is_gt,
                            fill=1.0, base=0, pattern=[[-1, 64]], channel_multiplier=1)
        g.affine_select(out=Bsel[:, 0:64], in_=Bsel[:, 0:64], compare_op=ALU.not_equal,
                        fill=0.5, base=-63, pattern=[[0, 64]], channel_multiplier=1)
        g.affine_select(out=Bsel[:, 64:128], in_=Bsel[:, 64:128], compare_op=ALU.not_equal,
                        fill=0.5, base=-127, pattern=[[0, 64]], channel_multiplier=1)
        g.affine_select(out=cind[:, 0:1], in_=cind[:, 0:1], compare_op=ALU.is_gt,
                        fill=1.0, base=-63, pattern=[[0, 1]], channel_multiplier=1)
        g.affine_select(out=cind[:, 1:2], in_=cind[:, 1:2], compare_op=ALU.is_ge,
                        fill=1.0, base=63, pattern=[[0, 1]], channel_multiplier=-1)
        ins = None
        for c in range(2):
            blk = maskE[64 * c:64 * c + 64, 64 * c:64 * c + 64]
            ins = g.affine_select(out=blk, in_=blk, compare_op=ALU.is_gt,
                                  fill=E2S0, base=0, pattern=[[-1, 64]], channel_multiplier=1)
        return ins

    def _stop(n, buf):
        if PHASE > n:
            return False
        vv = p.snap()
        p.emit("gpsimd", lambda g, buf=buf: g.dma_start(
            out=x1o[0:128, 0:buf.shape[1] if buf.shape[1] <= 1024 else 1024],
            in_=buf[:, 0:1024] if buf.shape[1] >= 1024 else buf[:]),
               wait=vv, dma="dso")
        p.emit("gpsimd", lambda g: g.nop(), wait=[("dso", p.val("dso"))])
        return True

    cm = p.emit("gpsimd", c_memsets)
    cstv = p.emit("gpsimd", c_fills, wait=[("G", cm)])
    cstv = p.reg_floats(es, [EPS, -0.5, -S0], "cstfA") or cstv

    W = {k: sb(k + "_sb", [128, 8 * 1024], BF16) for k in wts}
    row = sb("rowA", [1, 1024], F32)
    gB, bB, bwB = sb("gB", [128, 1024]), sb("bB", [128, 1024]), sb("bwB", [128, 1024])
    for k in wts:
        p.emit("gpsimd", lambda g, k=k: g.dma_start(
            out=W[k][:].rearrange("p (g d) -> p g d", g=8),
            in_=wts[k].rearrange("(g p) d -> p g d", p=128)), dma="dw")
    wldG = p.val("dw")

    xblk = sb("xblk", [128, 4 * 1024])
    stats = sb("statsA", [128, 48])
    xnbf = sb("xnbf", [128, 4 * 1024], BF16)
    xnT = sb("xnT", [128, 8 * 512], BF16)
    z1T = sb("z1T", [128, 8 * 512], BF16)
    labuf = sb("labuf", [128, 1024])
    escrA = sb("escrA", [128, 1024])
    escrB = sb("escrB", [128, 1024])
    escrC = sb("escrC", [128, 1024])
    escrR = sb("escrR", [128, 1024])
    Lsb = sb("LsbA", [128, 1024])
    Rqbf = sb("Rqbf", [128, 1024], BF16)
    Kdbf = sb("Kdbf", [128, 1024], BF16)
    Kec0 = sb("Kec0", [128, 1024], BF16)
    Kec1 = sb("Kec1", [128, 1024], BF16)
    Rxbf = sb("Rxbf", [128, 1024], BF16)
    RqT = sb("RqT", [128, 1024], BF16)
    KdT = sb("KdT", [128, 1024], BF16)
    RxT = sb("RxT", [128, 1024], BF16)
    AC = sb("ACA", [128, 16], F32)
    PTm = [sb(f"PTm{i}", [128, 128], BF16) for i in range(2)]
    S2 = sb("S2A", [128, 512], F32)
    Sblk = sb("SblkA", [128, 8 * 128], BF16)
    vbf = sb("vbfA", [128, 1024], BF16)
    x1b1 = sb("x1b", [128, 1024])
    x1b = [x1b1, x1b1]
    xrel = escrC
    outs = Kdbf
    outsT = RqT

    PB = [es.enter_context(nc.psum_tensor(f"pbA{i}", [128, 512], F32)) for i in range(6)]
    TB = [es.enter_context(nc.psum_tensor(f"tbA{i}", [128, 1024], BF16)) for i in range(2)]

    p.emit("vector", lambda v: v.memset(S2[:], 0.0))
    p.emit("vector", lambda v: v.memset(Sblk[:], 0.0))
    p.emit("vector", lambda v: v.memset(Kec0[:], 0.0))
    p.emit("vector", lambda v: v.memset(Kec1[:], 0.0))
    last_bc_v = 0
    for key, dst in (("L0_attn_ln_g", gB), ("L0_attn_ln_b", bB), ("L0_attn_bw", bwB)):
        ld = p.emit("sync", lambda s, key=key: s.dma_start(out=row[:], in_=vecs[key][None, :]),
                    wait=[("V", last_bc_v)], dma="drow")
        for half in range(2):
            tv = p.emit("tensor", lambda t, half=half: t.matmul(
                PB[0][:, :], ones1[:], row[:, 512 * half:512 * (half + 1)],
                start=True, stop=True),
                wait=[("drow", ld), ("G", cstv), ("V", last_bc_v)])
            last_bc_v = p.emit("vector", lambda v, dst=dst, half=half: v.tensor_copy(
                dst[:, 512 * half:512 * (half + 1)], PB[0][:, :]), wait=[("T", tv)])
    bcV = last_bc_v

    g_xn_v = 0
    g_xn_s = 0
    g_ttr = 0
    g_ez1 = 0
    last_mmfree = {}
    scan_T_done = 0
    last_ac = last_Lev = last_vsc = last_se = 0
    last_sctr = last_keu = last_vU1 = last_lach = 0
    last_vbf = 0
    prev_store = [0, 0]
    ev_ev_hist = [0, 0]
    lx = 0

    for j in range(NB):
        r0 = j * 512
        lx = p.emit("sync", lambda s, r0=r0: s.dma_start(
            out=xblk[:].rearrange("p (a d) -> p a d", a=4),
            in_=xin[r0:r0 + 512, :].rearrange("(a p) d -> p a d", p=128)),
            wait=[("V", g_xn_v), ("S", g_xn_s)], dma="dx")
        xt = [xblk[:, 1024 * i:1024 * (i + 1)] for i in range(4)]
        guard = [("dx", lx), ("V", bcV), ("V", last_vsc), ("S", last_se)]
        emit_ln(p, xt, xt, stats, 0, escrA, guard)
        ln2s = emit_ln(p, xt, xt, stats, 24, escrA, [("S", p.val("S"))])
        g_xn_s = ln2s
        for i in range(4):
            p.emit("vector", lambda v, i=i: v.tensor_mul(xt[i], xt[i], gB[:]),
                   wait=[("S", ln2s), ("T", g_ttr)])
            p.emit("vector", lambda v, i=i: v.tensor_add(xnbf[:, 1024 * i:1024 * (i + 1)],
                                                         xt[i], bB[:]))
        xnv = p.val("V")
        g_xn_v = xnv
        if _stop(2, xblk):
            return
        for i in range(4):
            tv = p.emit("tensor", lambda t, i=i: [
                t.transpose(TB[i % 2][:, 128 * dg:128 * (dg + 1)],
                            xnbf[:, 1024 * i + 128 * dg:1024 * i + 128 * (dg + 1)],
                            identT[:]) for dg in range(8)][-1],
                wait=[("V", xnv), ("G", cstv), ("V", ev_ev_hist[i % 2])])
            ev_ev_hist[i % 2] = p.emit("vector", lambda v, i=i: v.tensor_copy(
                xnT[:].rearrange("p (g t) -> p g t", g=8)[:, :, 128 * i:128 * (i + 1)],
                TB[i % 2][:].rearrange("p (g t) -> p g t", g=8)),
                wait=[("T", tv)])
        g_ttr = p.val("T")
        xne = p.val("V")
        if _stop(3, xblk):
            return

        ez1 = [g_ez1, g_ez1]
        for dg in range(8):
            tv = p.emit("tensor", lambda t, dg=dg: [
                t.matmul(PB[dg % 2][:, :],
                         W["L0_attn_Wx"][:, 1024 * cg + 128 * dg:1024 * cg + 128 * (dg + 1)],
                         xnT[:, 512 * cg:512 * (cg + 1)],
                         start=(cg == 0), stop=(cg == 7)) for cg in range(8)][-1],
                wait=[("V", xne), ("dw", wldG), ("S", ez1[dg % 2]),
                      ("V", last_vU1 if dg < 2 else 0), ("S", last_ac if dg < 2 else 0)])
            ez1[dg % 2] = p.emit("scalar", lambda s, dg=dg: s.copy(
                z1T[:, 512 * dg:512 * (dg + 1)], PB[dg % 2][:, :]), wait=[("T", tv)])
        g_ez1 = p.val("S")
        z1all = g_ez1
        if _stop(4, xblk):
            return

        for i in range(4):
            def emit_proj(mat, banks, waits, lhs, i=i):
                tvs = []
                for f in range(2):
                    def mm(t, f=f, mat=mat, lhs=lhs, i=i, banks=banks):
                        ins = None
                        for cg in range(8):
                            ins = t.matmul(
                                banks[f][:, :],
                                lhs[:, 512 * cg + 128 * i:512 * cg + 128 * (i + 1)],
                                W[mat][:, 1024 * cg + 512 * f:1024 * cg + 512 * (f + 1)],
                                start=(cg == 0), stop=(cg == 7))
                        return ins
                    tvs.append(p.emit("tensor", mm, wait=waits[f]))
                return tvs

            pA = (PB[2], PB[3])
            pB = (PB[4], PB[5])

            wz = [[("S", z1all), last_mmfree.get((2 + f), ("V", 0))] for f in range(2)]
            tz = emit_proj("L0_attn_Ww", pA, wz, z1T)
            zadds = []
            for f in range(2):
                zadds.append(p.emit("vector", lambda v, f=f: v.tensor_add(
                    labuf[:, 512 * f:512 * (f + 1)], pA[f][:, :], bwB[:, 512 * f:512 * (f + 1)]),
                    wait=[("T", tz[f]), ("S", last_lach), ("T", scan_T_done)]))
            for f in range(2):
                last_mmfree[2 + f] = ("V", zadds[f])
            zv = zadds[-1]
            p.emit("scalar", lambda s: s.activation(out=escrA[:], in_=labuf[:], func=AF.Abs),
                   wait=[("V", zv), ("V", last_vsc)])
            p.emit("scalar", lambda s: s.activation(out=escrB[:], in_=escrA[:], func=AF.Exp,
                                                    scale=-1.0))
            sC = p.emit("scalar", lambda s: s.activation(out=escrA[:], in_=escrB[:], func=AF.Ln,
                                                         bias=1.0))
            last_lach = sC
            p.emit("vector", lambda v: v.tensor_scalar_max(escrC[:], labuf[:], 0.0),
                   wait=[("V", zv)])
            law = p.emit("vector", lambda v: v.scalar_tensor_tensor(
                labuf[:], escrA[:], -1.0, escrC[:], ALU.mult, ALU.subtract),
                wait=[("S", sC)])

            if _stop(5, labuf):
                return
            tv = p.emit("tensor", lambda t: [
                t.matmul(PB[0][:, 2 * dg:2 * (dg + 1)],
                         labuf[:, 128 * dg:128 * (dg + 1)],
                         cind[:], start=(dg == 0), stop=(dg == 7),
                         skip_group_check=True) for dg in range(8)][-1],
                wait=[("V", law), ("S", last_ac), ("V", last_vU1), ("G", cstv)])
            acs = p.emit("scalar", lambda s: s.activation(
                out=AC[:], in_=PB[0][:, 0:16], func=AF.Exp), wait=[("T", tv)])
            last_ac = acs
            tL = p.emit("tensor", lambda t: [
                t.matmul(PB[2 + f][:, :], U64[:],
                         labuf[:, 512 * f:512 * (f + 1)],
                         start=True, stop=True) for f in range(2)][-1],
                wait=[("V", law), ("V", last_Lev), ("S", acs)])
            vL = p.emit("vector", lambda v: [
                v.tensor_copy(Lsb[:, 512 * f:512 * (f + 1)], PB[2 + f][:, :])
                for f in range(2)][-1], wait=[("T", tL), ("S", last_se)])
            last_Lev = vL
            last_mmfree[2] = last_mmfree[3] = ("V", vL)
            tM = p.emit("tensor", lambda t: [
                t.matmul(PB[4 + f][:, :], Bsel[:], Lsb[:, 512 * f:512 * (f + 1)],
                         start=True, stop=True) for f in range(2)][-1],
                wait=[("V", vL), last_mmfree.get("B_all", ("V", 0))])
            vE3 = p.emit("vector", lambda v: [
                v.scalar_tensor_tensor(labuf[:, 512 * f:512 * (f + 1)], PB[4 + f][:, :],
                                       -2.0, Lsb[:, 512 * f:512 * (f + 1)],
                                       ALU.mult, ALU.add) for f in range(2)][-1],
                wait=[("T", tM)])
            se4 = p.emit("scalar", lambda s: s.activation(out=escrR[:], in_=Lsb[:],
                                                          func=AF.Exp),
                         wait=[("V", vE3), ("V", last_vsc)])
            vE1 = p.emit("vector", lambda v: [
                v.tensor_tensor(Lsb[:, 512 * f:512 * (f + 1)], Lsb[:, 512 * f:512 * (f + 1)],
                                PB[4 + f][:, :], ALU.subtract) for f in range(2)][-1],
                wait=[("S", se4)])
            last_mmfree[4] = last_mmfree[5] = ("V", vE1)
            p.emit("scalar", lambda s: s.activation(out=escrA[:], in_=Lsb[:], func=AF.Exp,
                                                    bias=-S0),
                   wait=[("V", vE1), ("V", last_vsc)])
            p.emit("scalar", lambda s: s.activation(out=escrB[:], in_=Lsb[:], func=AF.Exp,
                                                    scale=-1.0, bias=-S0))
            se = p.emit("scalar", lambda s: s.activation(out=escrC[:], in_=labuf[:],
                                                         func=AF.Exp, scale=-1.0))
            last_se = se

            if _stop(6, Lsb):
                return
            wk = [[("V", xne), last_mmfree.get(2 + f, ("V", 0))] for f in range(2)]
            tk = emit_proj("L0_attn_Wk", pA, wk, xnT)
            kds = []
            for f in range(2):
                fs = slice(512 * f, 512 * (f + 1))
                p.emit("vector", lambda v, f=f, fs=fs: v.tensor_mul(
                    Kdbf[:, fs], pA[f][:, :], escrB[:, fs]),
                    wait=[("T", tk[f]), ("S", se), ("T", last_sctr), ("T", last_keu)])
                p.emit("vector", lambda v, f=f, fs=fs: v.tensor_mul(
                    Kec0[0:64, fs], pA[f][0:64, :], escrC[0:64, fs]))
                kds.append(p.emit("vector", lambda v, f=f, fs=fs: v.tensor_mul(
                    Kec1[64:128, fs], pA[f][64:128, :], escrC[64:128, fs])))
            for f in range(2):
                last_mmfree[2 + f] = ("V", kds[f])
            wr = [[("V", xne), last_mmfree.get(4 + f, ("V", 0))] for f in range(2)]
            tr = emit_proj("L0_attn_Wr", pB, wr, xnT)
            ers = []
            for f in range(2):
                ers.append(p.emit("scalar", lambda s, f=f: s.activation(
                    out=labuf[:, 512 * f:512 * (f + 1)], in_=pB[f][:, :], func=AF.Exp,
                    scale=-1.0),
                    wait=[("T", tr[f]), ("V", vE3), ("S", se)]))
            for f in range(2):
                last_mmfree[4 + f] = ("S", ers[f])
            p.emit("vector", lambda v: v.tensor_scalar_add(labuf[:], labuf[:], 1.0),
                   wait=[("S", ers[-1])])
            p.emit("vector", lambda v: v.reciprocal(labuf[:], labuf[:]))
            p.emit("vector", lambda v: v.tensor_mul(Rqbf[:], labuf[:], escrA[:]))
            rxv = p.emit("vector", lambda v: v.tensor_mul(Rxbf[:], labuf[:], escrR[:]))
            wv = [[("V", xne), last_mmfree.get(2 + f, ("V", 0))] for f in range(2)]
            tvv = emit_proj("L0_attn_Wv", pA, wv, xnT)
            vbs = []
            for f in range(2):
                vbs.append(p.emit("scalar", lambda s, f=f: s.copy(
                    vbf[:, 512 * f:512 * (f + 1)], pA[f][:, :]),
                    wait=[("T", tvv[f]), ("T", scan_T_done)]))
            for f in range(2):
                last_mmfree[2 + f] = ("S", vbs[f])
            last_vbf = vbs[-1]
            vsc = rxv
            last_vsc = p.val("V")

            if _stop(7, escrR):
                return
            tev = {}
            for nm, (srcb, dstb) in enumerate(((Rqbf, RqT), (Kdbf, KdT), (Rxbf, RxT))):
                tv = p.emit("tensor", lambda t, srcb=srcb, nm=nm: [
                    t.transpose(TB[nm % 2][:, 128 * dg:128 * (dg + 1)],
                                srcb[:, 128 * dg:128 * (dg + 1)], identT[:])
                    for dg in range(8)][-1],
                    wait=[("V", last_vsc), ("V", tev.get(nm - 2, 0)),
                          ("V", ev_ev_hist[nm % 2])])
                tev[nm] = p.emit("vector", lambda v, dstb=dstb, nm=nm: v.tensor_copy(
                    dstb[:], TB[nm % 2][:]), wait=[("T", tv)])
                ev_ev_hist[nm % 2] = tev[nm]
            last_sctr = p.val("T")
            trv = p.val("V")

            if _stop(8, escrR):
                return
            vPm = [0, 0]
            tO_last = 0
            for h in range(16):
                g2, half = h // 2, h % 2
                prow = slice(64 * half, 64 * half + 64)
                tcol = slice(128 * g2, 128 * g2 + 128)
                hch = slice(64 * h, 64 * h + 64)
                ptb = PB[h % 2]
                tP = p.emit("tensor", lambda t, prow=prow, tcol=tcol, ptb=ptb: t.matmul(
                    ptb[:, 0:128], KdT[prow, tcol], RqT[prow, tcol],
                    start=True, stop=True, skip_group_check=True),
                    wait=[("V", trv), ("V", vPm[h % 2]), ("S", acs)])
                vPm[h % 2] = p.emit("vector", lambda v, ptb=ptb, h=h: v.tensor_tensor(
                    PTm[h % 2][:], ptb[:, 0:128], maskE[:], ALU.mult),
                    wait=[("T", tP), ("G", cstv)])

                def tO(t, h=h, hch=hch):
                    ob = PB[2 + h // 8]
                    osl = slice(64 * (h % 8), 64 * (h % 8) + 64)
                    return t.matmul(ob[:, osl], PTm[h % 2][:], vbf[:, hch],
                                    start=(h % 8 == 0), stop=False, skip_group_check=True)

                if PHASE <= 9:
                    continue
                tO_last = p.emit("tensor", tO,
                                 wait=[("V", vPm[h % 2]), ("S", last_vbf), ("V", vL)])

            def tO2(t):
                for g2 in range(8):
                    ob = PB[2 + g2 // 4]
                    fr = slice(128 * (g2 % 4), 128 * (g2 % 4) + 128)
                    t.matmul(ob[0:64, fr], RxT[:, 128 * g2:128 * g2 + 64],
                             Sblk[:, 128 * g2:128 * g2 + 128],
                             start=False, stop=False, skip_group_check=True)
                ins = None
                for h in range(16):
                    half = h % 2
                    hch = slice(64 * h, 64 * h + 64)
                    ins = t.matmul(
                        PB[4][64 * half:64 * half + 64, 64 * (h // 2):64 * (h // 2) + 64],
                        Kec0[:, hch], vbf[:, hch],
                        start=(h < 2), stop=(h >= 14),
                        tile_position=(0, 64 * half), skip_group_check=True)
                return ins
            if PHASE > 9:
                tO_last = p.emit("tensor", tO2,
                                 wait=[("V", last_vU1), ("S", last_vbf), ("V", kds[-1]),
                                       ("V", vE1), ("S", ers[-1])])

            if PHASE <= 9:
                _stop(0, escrR)
                return
            if _stop(10, S2):
                return
            def vU0(v):
                ins = None
                for g2 in range(8):
                    fs = slice(64 * g2, 64 * g2 + 64)
                    ins = v.scalar_tensor_tensor(S2[:, fs], S2[:, fs],
                                                 AC[:, 2 * g2:2 * g2 + 1],
                                                 PB[4][:, fs], ALU.mult, ALU.add)
                return ins
            p.emit("vector", vU0, wait=[("T", tO_last), ("S", acs)])
            p.emit("vector", lambda v: v.tensor_copy(
                Sblk[0:64, :].rearrange("p (g c) -> p g c", g=8)[:, :, 0:64],
                S2[0:64, :].rearrange("p (g c) -> p g c", g=8)))
            vU0v = p.emit("vector", lambda v: v.tensor_copy(
                Sblk[64:128, :].rearrange("p (g c) -> p g c", g=8)[:, :, 64:128],
                S2[64:128, :].rearrange("p (g c) -> p g c", g=8)))

            def tC1(t):
                for g2 in range(8):
                    ob = PB[2 + g2 // 4]
                    fr = slice(128 * (g2 % 4), 128 * (g2 % 4) + 128)
                    t.matmul(ob[64:128, fr], RxT[:, 128 * g2 + 64:128 * g2 + 128],
                             Sblk[:, 128 * g2:128 * g2 + 128],
                             start=False, stop=(g2 % 4 == 3), tile_position=(0, 64),
                             skip_group_check=True)
                ins = None
                for h in range(16):
                    half = h % 2
                    hch = slice(64 * h, 64 * h + 64)
                    ins = t.matmul(
                        PB[5][64 * half:64 * half + 64, 64 * (h // 2):64 * (h // 2) + 64],
                        Kec1[:, hch], vbf[:, hch],
                        start=(h < 2), stop=(h >= 14),
                        tile_position=(0, 64 * half), skip_group_check=True)
                return ins
            tC1v = p.emit("tensor", tC1, wait=[("V", vU0v)])
            last_keu = tC1v
            if _stop(10.8, S2):
                return

            def vU1(v):
                ins = None
                for g2 in range(8):
                    fs = slice(64 * g2, 64 * g2 + 64)
                    ins = v.scalar_tensor_tensor(S2[:, fs], S2[:, fs],
                                                 AC[:, 2 * g2 + 1:2 * g2 + 2],
                                                 PB[5][:, fs], ALU.mult, ALU.add)
                return ins
            p.emit("vector", vU1, wait=[("T", tC1v)])
            p.emit("vector", lambda v: v.tensor_copy(
                Sblk[0:64, :].rearrange("p (g c) -> p g c", g=8)[:, :, 0:64],
                S2[0:64, :].rearrange("p (g c) -> p g c", g=8)))
            vU1v = p.emit("vector", lambda v: v.tensor_copy(
                Sblk[64:128, :].rearrange("p (g c) -> p g c", g=8)[:, :, 64:128],
                S2[64:128, :].rearrange("p (g c) -> p g c", g=8)))
            last_vU1 = vU1v
            last_mmfree[4] = last_mmfree[5] = ("V", vU1v)
            last_mmfree["B_all"] = ("V", vU1v)

            if _stop(11, S2):
                return
            xr = p.emit("sync", lambda s, r0=r0, i=i: s.dma_start(
                out=xrel[:], in_=xin[r0 + 128 * i:r0 + 128 * (i + 1), :]),
                wait=[("V", kds[-1])], dma="dxr")

            vO = p.emit("vector", lambda v: [
                v.tensor_copy(outs[:, 512 * b:512 * (b + 1)], PB[2 + b][:, :])
                for b in range(2)][-1], wait=[("T", tC1v)])
            tT = p.emit("tensor", lambda t: [
                t.transpose(TB[0][:, 128 * dg:128 * (dg + 1)],
                            outs[:, 128 * dg:128 * (dg + 1)], identT[:])
                for dg in range(8)][-1],
                wait=[("V", vO), ("V", ev_ev_hist[0])])
            vT = p.emit("vector", lambda v: v.tensor_copy(outsT[:], TB[0][:]),
                        wait=[("T", tT)])
            ev_ev_hist[0] = vT
            ty = p.emit("tensor", lambda t: [
                t.matmul(PB[2 + f][:, :],
                         outsT[:, 128 * cg:128 * (cg + 1)],
                         W["L0_attn_Wo"][:, 1024 * cg + 512 * f:1024 * cg + 512 * (f + 1)],
                         start=(cg == 0), stop=(cg == 7))
                for f in range(2) for cg in range(8)][-1],
                wait=[("V", vT), ("V", vO)])
            xi = x1b[0]
            vy = p.emit("vector", lambda v, xi=xi: [
                v.tensor_tensor(xi[:, 512 * f:512 * (f + 1)],
                                xrel[:, 512 * f:512 * (f + 1)],
                                PB[2 + f][:, :], ALU.add) for f in range(2)][-1],
                wait=[("T", ty), ("dxr", xr), ("dso", prev_store[0])])
            last_mmfree[2] = last_mmfree[3] = ("V", vy)
            prev_store[0] = p.emit(
                "gpsimd", lambda g, xi=xi, r0=r0, i=i: g.dma_start(
                    out=x1o[r0 + 128 * i:r0 + 128 * (i + 1), :], in_=xi[:]),
                wait=[("V", vy)], dma="dso")
        scan_T_done = p.val("T")

    p.emit("gpsimd", lambda g: g.nop(), wait=[("dso", p.val("dso"))])



def build_stageB(p, es, T, x1i, x2o):
    nc = p.nc
    E, F = 16, 4096
    keys = nc.dram_tensor("L0_ffn_keys", [E, D], F32, kind="ExternalInput").ap()
    rw = nc.dram_tensor("L0_ffn_rw", [E, D], F32, kind="ExternalInput").ap()
    rb = nc.dram_tensor("L0_ffn_rb", [E], F32, kind="ExternalInput").ap()
    w1 = nc.dram_tensor("L0_ffn_w1", [E, F, D], F32, kind="ExternalInput").ap()
    b1 = nc.dram_tensor("L0_ffn_b1", [E, F], F32, kind="ExternalInput").ap()
    w2 = nc.dram_tensor("L0_ffn_w2", [E, D, F], F32, kind="ExternalInput").ap()
    b2 = nc.dram_tensor("L0_ffn_b2", [E, D], F32, kind="ExternalInput").ap()
    w1Tc = nc.dram_tensor("B_w1Tc", [16, 4, 128, 8192], BF16, kind="Internal").ap()
    w2Tc = nc.dram_tensor("B_w2Tc", [16, 4, 128, 8192], BF16, kind="Internal").ap()
    TH = T // 2
    NTT = TH // 128
    NBK = TH // 256
    sb = lambda name, shape, dt=F32: es.enter_context(nc.sbuf_tensor(name, shape, dt))

    identT = sb("B_identT", [128, 128], BF16)
    identF = sb("B_identF", [128, 128], F32)
    ones1 = sb("B_ones1", [1, 128], F32)
    onesbf = sb("B_onesbf", [1, 512], BF16)

    def c_memsets(g):
        g.memset(identT[:], 0.0)
        g.memset(identF[:], 0.0)
        g.memset(onesbf[:], 1.0)
        g.memset(onec[:], 1.0)
        return g.memset(ones1[:], 1.0)

    def c_fills(g):
        g.affine_select(out=identT[:], in_=identT[:], compare_op=ALU.not_equal,
                        fill=1.0, base=0, pattern=[[-1, 128]], channel_multiplier=1)
        return g.affine_select(out=identF[:], in_=identF[:], compare_op=ALU.not_equal,
                               fill=1.0, base=0, pattern=[[-1, 128]], channel_multiplier=1)

    cm = p.emit("gpsimd", c_memsets)
    cstv = p.emit("gpsimd", c_fills, wait=[("G", cm)])
    cstv = p.reg_floats(es, [EPS, -0.5], "B_cstf") or cstv

    PR = [es.enter_context(nc.psum_tensor(f"B_pr{i}", [128, 512], F32)) for i in range(2)]
    PH = [es.enter_context(nc.psum_tensor(f"B_ph{i}", [128, 512], F32)) for i in range(2)]
    PO = [es.enter_context(nc.psum_tensor(f"B_po{i}", [128, 512], F32)) for i in range(2)]
    TB = [es.enter_context(nc.psum_tensor(f"B_tb{i}", [128, 1024], BF16)) for i in range(2)]

    keysb = sb("B_keys", [16, 1024], F32)
    rwb = sb("B_rw", [16, 1024], F32)
    Cbf = sb("B_Cbf", [16, 1024], BF16)
    CT = sb("B_CT", [128, 8 * 16], BF16)
    b2sb = sb("B_b2", [16, 1024], BF16)
    ks2 = sb("B_ks2", [16, 1], F32)
    blog = sb("B_blog", [1, 16], F32)
    rbrow = sb("B_rbrow", [1, 16], F32)
    b1row = sb("B_b1row", [1, 1024], BF16)
    b1col = sb("B_b1col", [128, 8], F32)
    onec = sb("B_onec", [1, 1], BF16)

    l1 = p.emit("sync", lambda s: s.dma_start(out=keysb[:], in_=keys[:, :]), dma="drB")
    l2 = p.emit("sync", lambda s: s.dma_start(out=rwb[:], in_=rw[:, :]), dma="drB")
    l3 = p.emit("sync", lambda s: s.dma_start(out=rbrow[:], in_=rb[None, :]), dma="drB")
    l4 = p.emit("gpsimd", lambda g: g.dma_start(out=b2sb[:], in_=b2[:, :]), dma="dwB")
    v1 = p.emit("vector", lambda v: v.scalar_tensor_tensor(
        Cbf[:], keysb[:], 2.0, rwb[:], ALU.mult, ALU.add), wait=[("drB", l2)])
    p.emit("vector", lambda v: v.tensor_mul(rwb[:], keysb[:], keysb[:]),
           wait=[("drB", l2)])
    v2 = p.emit("vector", lambda v: v.reduce_sum(ks2[:], rwb[:], AX))
    tC = p.emit("tensor", lambda t: [
        t.transpose(TB[0][:, 16 * dg:16 * (dg + 1)],
                    Cbf[:, 128 * dg:128 * (dg + 1)], identT[0:16, 0:16])
        for dg in range(8)][-1], wait=[("V", v1), ("G", cstv)])
    vC = p.emit("vector", lambda v: v.tensor_copy(CT[:], TB[0][:, 0:128]),
                wait=[("T", tC)])
    tk2 = p.emit("tensor", lambda t: t.transpose(
        PH[0][0:1, 0:16], ks2[:], identF[0:16, 0:16]),
        wait=[("V", v2), ("G", cstv)])
    vk2 = p.emit("vector", lambda v: v.scalar_tensor_tensor(
        blog[:], PH[0][0:1, 0:16], -1.0, rbrow[:], ALU.mult, ALU.add),
        wait=[("T", tk2), ("drB", l3)])
    blogB = sb("B_blogB", [128, 16], F32)
    tbb = p.emit("tensor", lambda t: t.matmul(
        PH[1][:, 0:16], ones1[:], blog[:], start=True, stop=True),
        wait=[("V", vk2)])
    vbb = p.emit("vector", lambda v: v.tensor_copy(blogB[:], PH[1][:, 0:16]),
                 wait=[("T", tbb)])

    xblk = sb("B_xblk", [128, 2 * 1024])
    xn2bf = sb("B_xn2bf", [128, 4 * 1024], BF16)
    xn2T = sb("B_xn2T", [128, 8 * 2048], BF16)
    accb = sb("B_acc", [128, 16 * 1024])
    hidT = sb("B_hidT", [128, 8 * 2048], BF16)
    w1T = sb("B_w1T", [128, 8 * 1024], BF16)
    w2T = sb("B_w2T", [128, 8 * 1024], BF16)
    wstg = [sb(f"B_wstg{i}", [128, 1024], BF16) for i in range(4)]
    stats = sb("B_stats", [128, 32])
    gatew = sb("B_gatew", [128, 32 * 16])
    lg = sb("B_lg", [128, 16])
    l2b = sb("B_l2b", [128, 16])
    exb = sb("B_exb", [128, 16])
    cols = sb("B_cols", [128, 8])
    gTbf = sb("B_gTbf", [16, 128], BF16)

    ev_hist = [vC, 0]
    stg_T = [0, 0, 0, 0]
    ln_s_prev = 0
    last_xn_v = 0
    last_vai = 0
    last_sx = 0
    acc_free = 0
    tpos_guard = 0

    pp_store = {0: 0, 1: 0}
    for e in range(E):
        for fc in range(4):
            for wi, (wsrc, wcache) in enumerate(((w1, w1Tc), (w2, w2Tc))):
                wTbuf = w1T if wi == 0 else w2T
                for s8 in range(8):
                    sl4 = s8 % 4
                    if wi == 0:
                        ld = p.emit("gpsimd", lambda g, e=e, fc=fc, s8=s8, sl4=sl4:
                            g.dma_start(
                                out=wstg[sl4][:],
                                in_=w1[e, 1024 * fc + 128 * s8:1024 * fc + 128 * (s8 + 1), :]),
                            wait=[("T", stg_T[sl4])], dma=f"dw1{sl4}")
                    else:
                        ld = p.emit("gpsimd", lambda g, e=e, fc=fc, s8=s8, sl4=sl4:
                            g.dma_start(
                                out=wstg[sl4][:],
                                in_=w2[e, 128 * s8:128 * (s8 + 1), 1024 * fc:1024 * (fc + 1)]),
                            wait=[("T", stg_T[sl4])], dma=f"dw1{sl4}")
                    tv = p.emit("tensor", lambda t, s8=s8, sl4=sl4: [
                        t.transpose(TB[s8 % 2][:, 128 * dg:128 * (dg + 1)],
                                    wstg[sl4][:, 128 * dg:128 * (dg + 1)], identT[:])
                        for dg in range(8)][-1],
                        wait=[(f"dw1{sl4}", ld), ("V", ev_hist[s8 % 2])])
                    stg_T[sl4] = tv
                    ev_hist[s8 % 2] = p.emit("vector", lambda v, s8=s8, wTbuf=wTbuf:
                        v.tensor_copy(
                            wTbuf[:].rearrange("p (g f) -> p g f", g=8)
                            [:, :, 128 * s8:128 * (s8 + 1)],
                            TB[s8 % 2][:].rearrange("p (g f) -> p g f", g=8)),
                        wait=[("T", tv), ("dwc", pp_store[wi])])
                pp_store[wi] = p.emit("gpsimd", lambda g, e=e, fc=fc, wcache=wcache,
                                      wTbuf=wTbuf: g.dma_start(
                    out=wcache[e, fc], in_=wTbuf[:]),
                    wait=[("V", ev_hist[0]), ("V", ev_hist[1])], dma="dwc")
    pp_done = p.val("dwc")

    for th in range(2):
        t0g = th * TH
        half_T0 = p.val("T")
        half_S0 = p.val("S")
        for bk in range(NBK):
            r0 = t0g + bk * 256
            lx = p.emit("sync", lambda s, r0=r0: s.dma_start(
                out=xblk[:].rearrange("p (a d) -> p a d", a=2),
                in_=x1i[r0:r0 + 256, :].rearrange("(a p) d -> p a d", p=128)),
                wait=[("S", ln_s_prev), ("V", last_vai), ("V", acc_free)], dma="dxB")
            xt = [xblk[:, 1024 * i:1024 * (i + 1)] for i in range(2)]
            ln_s_prev = emit_ln(p, xt, xt, stats, 0, xn2bf[:, 0:1024],
                                [("dxB", lx), ("G", cstv), ("V", last_xn_v),
                                 ("T", tpos_guard)])
            c_s2, c_mu, c_rs = stats[:, 2:4], stats[:, 4:6], stats[:, 6:8]
            c_t, c_s2n = stats[:, 20:22], stats[:, 22:24]
            p.emit("vector", lambda v: v.tensor_mul(c_t[:], c_mu[:], c_mu[:]),
                   wait=[("S", ln_s_prev)])
            p.emit("vector", lambda v: v.scalar_tensor_tensor(
                c_t[:], c_t[:], -float(D), c_s2[:], ALU.mult, ALU.add))
            p.emit("vector", lambda v: v.tensor_mul(c_s2n[:], c_rs[:], c_rs[:]))
            vs2n = p.emit("vector", lambda v: v.tensor_mul(c_s2n[:], c_s2n[:], c_t[:]))
            for i in range(2):
                p.emit("vector", lambda v, i=i: v.tensor_copy(
                    xn2bf[:, 1024 * i:1024 * (i + 1)], xt[i]),
                    wait=[("S", ln_s_prev), ("T", tpos_guard), ("V", acc_free)])
            xnv = p.val("V")
            last_xn_v = xnv
            for i in range(2):
                ti_loc = bk * 2 + i
                ti_abs = th * NTT + ti_loc
                tv = p.emit("tensor", lambda t, i=i: [
                    t.transpose(TB[i % 2][:, 128 * dg:128 * (dg + 1)],
                                xn2bf[:, 1024 * i + 128 * dg:1024 * i + 128 * (dg + 1)],
                                identT[:]) for dg in range(8)][-1],
                    wait=[("V", xnv), ("V", ev_hist[i % 2]), ("G", cstv),
                          ("T", half_T0)])
                ev_hist[i % 2] = p.emit("vector", lambda v, i=i, ti_loc=ti_loc: v.tensor_copy(
                    xn2T[:].rearrange("p (g t) -> p g t", g=8)[:, :, 128 * ti_loc:128 * (ti_loc + 1)],
                    TB[i % 2][:].rearrange("p (g t) -> p g t", g=8)),
                    wait=[("T", tv)])
                trt = p.emit("tensor", lambda t, ti_loc=ti_loc: [
                    t.matmul(PR[0][:, 0:16],
                             xn2T[:, 2048 * cg + 128 * ti_loc:2048 * cg + 128 * (ti_loc + 1)],
                             CT[:, 16 * cg:16 * (cg + 1)],
                             start=(cg == 0), stop=(cg == 7)) for cg in range(8)][-1],
                    wait=[("V", ev_hist[i % 2]), ("V", vC), ("V", last_vai)])
                vlg = p.emit("vector", lambda v, i=i: v.scalar_tensor_tensor(
                    lg[:], PR[0][:, 0:16], c_s2n[:, i:i + 1], blogB[:],
                    ALU.subtract, ALU.add),
                    wait=[("T", trt), ("V", vs2n), ("S", last_sx), ("V", vbb)])
                p.emit("vector", lambda v: v.tensor_copy(l2b[:], lg[:]))
                p.emit("vector", lambda v: v.reduce_max(cols[:, 0:1], lg[:], AX))
                for r in range(3):
                    p.emit("vector", lambda v: v.reduce_max(cols[:, 1:2], l2b[:], AX))
                    p.emit("vector", lambda v: v.tensor_scalar(
                        exb[:], l2b[:], cols[:, 1:2], None, ALU.is_ge))
                    p.emit("vector", lambda v: v.scalar_tensor_tensor(
                        l2b[:], exb[:], -1e30, l2b[:], ALU.mult, ALU.add))
                p.emit("vector", lambda v: v.reduce_max(cols[:, 2:3], l2b[:], AX))
                vm1 = p.emit("vector", lambda v: v.tensor_scalar_mul(
                    cols[:, 3:4], cols[:, 0:1], -1.0))
                last_sx = p.emit("scalar", lambda s: s.activation(
                    out=exb[:], in_=lg[:], func=AF.Exp, bias=cols[:, 3:4]),
                    wait=[("V", vm1)])
                p.emit("vector", lambda v: v.tensor_scalar(
                    l2b[:], lg[:], cols[:, 2:3], None, ALU.is_ge), wait=[("S", last_sx)])
                p.emit("vector", lambda v: v.tensor_mul(exb[:], exb[:], l2b[:]))
                p.emit("vector", lambda v: v.reduce_sum(cols[:, 4:5], exb[:], AX))
                p.emit("vector", lambda v: v.reciprocal(cols[:, 5:6], cols[:, 4:5]))
                vgw = p.emit("vector", lambda v, ti_abs=ti_abs: v.tensor_scalar(
                    gatew[:, 16 * ti_abs:16 * (ti_abs + 1)], exb[:], cols[:, 5:6], None,
                    ALU.mult))
                tgT = p.emit("tensor", lambda t, ti_abs=ti_abs: t.transpose(
                    PH[0][0:16, 0:128], gatew[:, 16 * ti_abs:16 * (ti_abs + 1)],
                    identF[:]), wait=[("V", vgw), ("S", half_S0 if bk + i == 0 else 0),
                                      ("V", acc_free)])
                vgT = p.emit("vector", lambda v: v.tensor_copy(gTbf[:], PH[0][0:16, 0:128]),
                             wait=[("T", tgT)])
                tb2 = p.emit("tensor", lambda t: [
                    t.matmul((PR[1] if fh else PR[0])[:, :], gTbf[:],
                             b2sb[:, 512 * fh:512 * (fh + 1)], start=True, stop=True)
                    for fh in range(2)][-1],
                    wait=[("V", vgT), ("dwB", l4), ("V", last_vai)])
                lxr = p.emit("gpsimd", lambda g, ti_loc=ti_loc, ti_abs=ti_abs: g.dma_start(
                    out=accb[:, 1024 * ti_loc:1024 * (ti_loc + 1)],
                    in_=x1i[t0g + 128 * (ti_abs - th * NTT):t0g + 128 * (ti_abs - th * NTT) + 128, :]),
                    wait=[("V", acc_free), ("V", last_vai)], dma="dxB2")
                last_vai = p.emit("vector", lambda v, ti_loc=ti_loc: [
                    v.tensor_tensor(
                        accb[:, 1024 * ti_loc + 512 * fh:1024 * ti_loc + 512 * (fh + 1)],
                        accb[:, 1024 * ti_loc + 512 * fh:1024 * ti_loc + 512 * (fh + 1)],
                        (PR[1] if fh else PR[0])[:, :], ALU.add)
                    for fh in range(2)][-1],
                    wait=[("T", tb2), ("dxB2", lxr)])
            tpos_guard = p.val("T")

        hg = [0, 0]
        og = [0, 0]
        unit_T_guard = p.val("T")
        w1_used = p.val("T")
        for e in range(E):
            for fc in range(4):
                lb1 = p.emit("gpsimd", lambda g, e=e, fc=fc: g.dma_start(
                    out=b1row[:], in_=b1[e, 1024 * fc:1024 * (fc + 1)][None, :]),
                    wait=[("T", p.val("T"))], dma="dwB")
                tb1 = p.emit("tensor", lambda t: [
                    t.matmul(PH[0][:, s8:s8 + 1], b1row[:, 128 * s8:128 * (s8 + 1)],
                             onec[:], start=(s8 == 0), stop=(s8 == 7))
                    for s8 in range(8)][-1],
                    wait=[("dwB", lb1), ("S", hg[0]), ("S", hg[1])])
                vb1 = p.emit("vector", lambda v: v.tensor_copy(b1col[:], PH[0][:, 0:8]),
                             wait=[("T", tb1)])
                w1all = p.emit("sync", lambda s, e=e, fc=fc: s.dma_start(
                    out=w1T[:], in_=w1Tc[e, fc]),
                    wait=[("T", w1_used), ("dwc", pp_done)], dma="dwT1")
                for s8 in range(8):
                    for tt in range(TH // 512):
                        bsel = (s8 * 4 + tt) % 2
                        hb = PH[bsel]
                        tv = p.emit("tensor", lambda t, s8=s8, tt=tt, hb=hb: [
                            t.matmul(hb[:, :],
                                     w1T[:, 1024 * cg + 128 * s8:1024 * cg + 128 * (s8 + 1)],
                                     xn2T[:, 2048 * cg + 512 * tt:2048 * cg + 512 * (tt + 1)],
                                     start=(cg == 0), stop=(cg == 7)) for cg in range(8)][-1],
                            wait=[("dwT1", w1all), ("S", hg[bsel]), ("V", vb1)])
                        hg[bsel] = p.emit("scalar", lambda s, s8=s8, tt=tt, hb=hb: s.activation(
                            out=hidT[:, 2048 * s8 + 512 * tt:2048 * s8 + 512 * (tt + 1)],
                            in_=hb[:, :], func=AF.Gelu, bias=b1col[:, s8:s8 + 1]),
                            wait=[("T", tv), ("V", vb1),
                                  ("T", unit_T_guard if (s8, tt) == (0, 0) else 0)])
                hidall = p.val("S")
                w1_used = p.val("T")
                w2all = p.emit("sync", lambda s, e=e, fc=fc: s.dma_start(
                    out=w2T[:], in_=w2Tc[e, fc]),
                    wait=[("T", unit_T_guard)], dma="dwT2")
                for tt in range(NTT):
                    ti_abs = th * NTT + tt
                    for dh in range(2):
                        bsel = (tt * 2 + dh) % 2
                        ob = PO[bsel]
                        tv = p.emit("tensor", lambda t, tt=tt, dh=dh, ob=ob: [
                            t.matmul(ob[:, :],
                                     hidT[:, 2048 * fs + 128 * tt:2048 * fs + 128 * (tt + 1)],
                                     w2T[:, 1024 * fs + 512 * dh:1024 * fs + 512 * (dh + 1)],
                                     start=(fs == 0), stop=(fs == 7)) for fs in range(8)][-1],
                            wait=[("dwT2", w2all), ("S", hidall), ("V", og[bsel])])
                        og[bsel] = p.emit("vector", lambda v, tt=tt, dh=dh, ob=ob, e=e,
                                          ti_abs=ti_abs: v.scalar_tensor_tensor(
                            accb[:, 1024 * tt + 512 * dh:1024 * tt + 512 * (dh + 1)],
                            ob[:, :], gatew[:, 16 * ti_abs + e:16 * ti_abs + e + 1],
                            accb[:, 1024 * tt + 512 * dh:1024 * tt + 512 * (dh + 1)],
                            ALU.mult, ALU.add),
                            wait=[("T", tv), ("V", last_vai if e == 0 else 0)])
                unit_T_guard = p.val("T")
        stv = 0
        for tt in range(NTT):
            r0 = t0g + 128 * tt
            stv = p.emit("gpsimd", lambda g, tt=tt, r0=r0: g.dma_start(
                out=x2o[r0:r0 + 128, :], in_=accb[:, 1024 * tt:1024 * (tt + 1)]),
                wait=[("V", og[0]), ("V", og[1])], dma="dsoB")
        acc_free = p.emit("vector", lambda v: v.engine_nop(), wait=[("dsoB", stv)])
    p.emit("gpsimd", lambda g: g.nop(), wait=[("dsoB", p.val("dsoB"))])



def build_stageC(p, es, T, x2i, x3o):
    nc = p.nc
    NH, HD = 8, 128
    AEPS = 1e-6
    wts = {}
    wts["Wq"] = nc.dram_tensor("L1_attn_Wq", [D, D], F32, kind="ExternalInput").ap()
    wts["Wm"] = nc.dram_tensor("L1_attn_Wm", [D, D], F32, kind="ExternalInput").ap()
    wts["W1"] = nc.dram_tensor("L1_attn_W1", [D, 2 * D], F32, kind="ExternalInput").ap()
    wts["W2"] = nc.dram_tensor("L1_attn_W2", [2 * D, D], F32, kind="ExternalInput").ap()
    vecs = {k: nc.dram_tensor("L1_attn_" + k, [D], F32, kind="ExternalInput").ap()
            for k in ("g1", "b1", "g2", "b2")}
    NB = T // 256
    sb = lambda name, shape, dt=F32: es.enter_context(nc.sbuf_tensor(name, shape, dt))

    identT = sb("C_identT", [128, 128], BF16)
    ones1 = sb("C_ones1", [1, 128], F32)
    onescol = sb("C_onescol", [128, 1], BF16)

    def c_memsets(g):
        g.memset(identT[:], 0.0)
        g.memset(onescol[:], 1.0)
        return g.memset(ones1[:], 1.0)

    def c_fills(g):
        return g.affine_select(out=identT[:], in_=identT[:], compare_op=ALU.not_equal,
                               fill=1.0, base=0, pattern=[[-1, 128]], channel_multiplier=1)

    cm = p.emit("gpsimd", c_memsets)
    cstv = p.emit("gpsimd", c_fills, wait=[("G", cm)])
    cstv = p.reg_floats(es, [EPS, -0.5], "C_cstf") or cstv

    PBa = [es.enter_context(nc.psum_tensor(f"C_pa{i}", [128, 512], F32)) for i in range(2)]
    PBb = [es.enter_context(nc.psum_tensor(f"C_pb{i}", [128, 512], F32)) for i in range(2)]
    PKV = [es.enter_context(nc.psum_tensor(f"C_pkv{i}", [128, 512], F32)) for i in range(2)]
    TB = [es.enter_context(nc.psum_tensor(f"C_tb{i}", [128, 1024], BF16)) for i in range(2)]

    Wq = sb("C_Wq", [128, 8 * 1024], BF16)
    Wm = sb("C_Wm", [128, 8 * 1024], BF16)
    W1 = sb("C_W1", [128, 8 * 2048], BF16)
    W2 = sb("C_W2", [128, 16 * 1024], BF16)
    for k, buf, gdim in (("Wq", Wq, 8), ("Wm", Wm, 8), ("W1", W1, 8), ("W2", W2, 16)):
        p.emit("gpsimd", lambda g, k=k, buf=buf, gdim=gdim: g.dma_start(
            out=buf[:].rearrange("p (g d) -> p g d", g=gdim),
            in_=wts[k].rearrange("(g p) d -> p g d", p=128)), dma="dwC")
    wldC = p.val("dwC")
    row = sb("C_row", [1, 1024], F32)
    gbB = {k: sb(f"C_{k}B", [128, 1024]) for k in ("g1", "b1", "g2", "b2")}
    last_bc = 0
    for k in ("g1", "b1", "g2", "b2"):
        ld = p.emit("sync", lambda s, k=k: s.dma_start(out=row[:], in_=vecs[k][None, :]),
                    wait=[("V", last_bc)], dma="drC")
        for half in range(2):
            tv = p.emit("tensor", lambda t, half=half: t.matmul(
                PBa[0][:, :], ones1[:], row[:, 512 * half:512 * (half + 1)],
                start=True, stop=True),
                wait=[("drC", ld), ("G", cstv), ("V", last_bc)])
            last_bc = p.emit("vector", lambda v, k=k, half=half: v.tensor_copy(
                gbB[k][:, 512 * half:512 * (half + 1)], PBa[0][:, :]), wait=[("T", tv)])
    bcV = last_bc

    xblk = sb("C_xblk", [128, 2 * 1024])
    lnbf = sb("C_lnbf", [128, 2 * 1024], BF16)
    TshT = sb("C_TshT", [128, 4 * 512], BF16)
    featbf = sb("C_featbf", [128, 2 * 1024], BF16)
    vqbf = sb("C_vqbf", [128, 2 * 1024], BF16)
    scr1 = sb("C_scr1", [128, 1024])
    scr2 = sb("C_scr2", [128, 1024])
    scrF = sb("C_scrF", [128, 1024])
    msgsbf = sb("C_msgsbf", [128, 2 * 1024], BF16)
    msgobf = sb("C_msgobf", [128, 2 * 1024], BF16)
    h1T = sb("C_h1T", [128, 16 * 256], BF16)
    x3b = sb("C_x3b", [128, 2 * 1024])
    xrel = sb("C_xrel", [128, 1024])
    stats = sb("C_stats", [128, 16])
    KVbf = sb("C_KVbf", [128, 8 * 128], BF16)
    KsM = sb("C_KsM", [128, 8 * 8], BF16)
    Zb = sb("C_Zb", [128, 16])

    ev_hist = [bcV, 0]
    ln_s_prev = 0
    last_feat_v = 0
    last_xn_v = 0
    guardsC = {"ln2x": 0, "scr1": 0}

    def emit_block_lnq(bk, x_src, phase, consume):
        nonlocal ln_s_prev, last_xn_v
        r0 = bk * 256
        lx = p.emit("sync", lambda s, r0=r0: s.dma_start(
            out=xblk[:].rearrange("p (a d) -> p a d", a=2),
            in_=x_src[r0:r0 + 256, :].rearrange("(a p) d -> p a d", p=128)),
            wait=[("S", ln_s_prev), ("V", last_xn_v), ("V", guardsC["ln2x"])], dma="dxC")
        xt = [xblk[:, 1024 * i:1024 * (i + 1)] for i in range(2)]
        ln_s_prev = emit_ln(p, xt, xt, stats, 0, scr1,
                            [("dxC", lx), ("G", cstv), ("V", bcV), ("V", last_feat_v),
                             ("V", guardsC["scr1"]),
                             ("T", p.val("T") if (bk == 0 and phase == "B") else 0)])
        for i in range(2):
            p.emit("vector", lambda v, i=i: v.tensor_copy(
                lnbf[:, 1024 * i:1024 * (i + 1)], xt[i]),
                wait=[("S", ln_s_prev)])
        xnv = p.val("V")
        last_xn_v = xnv
        for i in range(2):
            tv = p.emit("tensor", lambda t, i=i: [
                t.transpose(TB[i % 2][:, 128 * dg:128 * (dg + 1)],
                            lnbf[:, 1024 * i + 128 * dg:1024 * i + 128 * (dg + 1)],
                            identT[:]) for dg in range(8)][-1],
                wait=[("V", xnv), ("V", ev_hist[i % 2]), ("G", cstv)])
            ev_hist[i % 2] = p.emit("vector", lambda v, i=i: v.tensor_copy(
                TshT[:].rearrange("p (g t) -> p g t", g=8)[:, :, 128 * i:128 * (i + 1)],
                TB[i % 2][:].rearrange("p (g t) -> p g t", g=8)),
                wait=[("T", tv)])
        lnTv = p.val("V")
        for i in range(2):
            for f in range(2):
                bank = (PBa if (phase == "A" or i % 2 == 0) else PBb)[f]
                key = (0, f) if phase == "A" else (i % 2, f)
                tv = p.emit("tensor", lambda t, i=i, f=f, bank=bank: [
                    t.matmul(bank[:, :],
                             TshT[:, 256 * cg + 128 * i:256 * cg + 128 * (i + 1)],
                             Wq[:, 1024 * cg + 512 * f:1024 * cg + 512 * (f + 1)],
                             start=(cg == 0), stop=(cg == 7)) for cg in range(8)][-1],
                    wait=[("V", lnTv), ("dwC", wldC)] + list(qfree[key]))
                qfree[key] = consume(i, f, bank, tv)

    qfree = {(a, b): [] for a in range(2) for b in range(2)}
    last_kv = 0
    def chainA(i, f, bank, tv):
        hf = slice(512 * f, 512 * (f + 1))
        sv = p.emit("scalar", lambda s, i=i, bank=bank, hf=hf: s.copy(
            vqbf[:, 1024 * i:1024 * (i + 1)][:, hf], bank[:, :]),
            wait=[("T", tv), ("T", last_kv)])
        p.emit("vector", lambda v, bank=bank, hf=hf: v.tensor_scalar_min(
            scr1[:, hf], bank[:, :], 0.0), wait=[("T", tv), ("S", p.val("S"))])
        se_ = p.emit("scalar", lambda s, hf=hf: s.activation(
            out=scr2[:, hf], in_=scr1[:, hf], func=AF.Exp),
            wait=[("V", p.val("V"))])
        fv2 = p.emit("vector", lambda v, bank=bank, hf=hf: v.tensor_scalar_max(
            scr1[:, hf], bank[:, :], 0.0), wait=[("S", se_)])
        fv = p.emit("vector", lambda v, i=i, hf=hf: v.tensor_add(
            featbf[:, 1024 * i:1024 * (i + 1)][:, hf], scr1[:, hf], scr2[:, hf]),
            wait=[("T", last_kv)])
        return [("V", fv2), ("S", sv)]

    for bk in range(NB):
        emit_block_lnq(bk, x2i, "A", chainA)
        featv = p.val("V")
        svv = p.val("S")
        def kvmm(t, bk=bk):
            ins = None
            for i in range(2):
                for h in range(8):
                    first = (bk == 0 and i == 0 and h % 4 == 0)
                    ins = t.matmul(
                        PKV[h // 4][:, 128 * (h % 4):128 * (h % 4) + 128],
                        featbf[:, 1024 * i + 128 * h:1024 * i + 128 * (h + 1)],
                        vqbf[:, 1024 * i + 128 * h:1024 * i + 128 * (h + 1)],
                        start=first, stop=(bk == NB - 1 and i == 1 and h % 4 == 3),
                        skip_group_check=True)
                    ins = t.matmul(
                        PBb[0][:, h:h + 1],
                        featbf[:, 1024 * i + 128 * h:1024 * i + 128 * (h + 1)],
                        onescol[:],
                        start=(bk == 0 and i == 0 and h == 0),
                        stop=(bk == NB - 1 and i == 1 and h == 7),
                        skip_group_check=True)
            return ins
        last_kv = p.emit("tensor", kvmm, wait=[("V", featv), ("S", svv)])
        last_feat_v = featv
    vkv = p.emit("vector", lambda v: [
        v.tensor_copy(KVbf[:, 512 * b:512 * (b + 1)], PKV[b][:, :]) for b in range(2)][-1],
        wait=[("T", last_kv)])
    p.emit("vector", lambda v: v.memset(KsM[:], 0.0))
    vks = p.emit("vector", lambda v: [
        v.tensor_copy(KsM[:, 8 * dg + dg:8 * dg + dg + 1], PBb[0][:, dg:dg + 1])
        for dg in range(8)][-1])
    kv_ready = p.val("V")

    qfree = {(a, b): [("V", kv_ready)] for a in range(2) for b in range(2)}
    st_prev = 0
    last_mo_s = 0
    last_vx3 = 0
    last_fTt = 0
    last_moT = 0
    def chainB(i, f, bank, tv):
        hf = slice(512 * f, 512 * (f + 1))
        p.emit("vector", lambda v, bank=bank, hf=hf: v.tensor_scalar_min(
            scr1[:, hf], bank[:, :], 0.0), wait=[("T", tv), ("S", p.val("S"))])
        se_ = p.emit("scalar", lambda s, hf=hf: s.activation(
            out=scr2[:, hf], in_=scr1[:, hf], func=AF.Exp),
            wait=[("V", p.val("V"))])
        fv2 = p.emit("vector", lambda v, bank=bank, hf=hf: v.tensor_scalar_max(
            scr1[:, hf], bank[:, :], 0.0), wait=[("S", se_)])
        p.emit("vector", lambda v, i=i, hf=hf: v.tensor_add(
            featbf[:, 1024 * i:1024 * (i + 1)][:, hf], scr1[:, hf], scr2[:, hf]),
            wait=[("T", last_fTt)])
        return [("V", fv2)]

    for bk in range(NB):
        r0 = bk * 256
        emit_block_lnq(bk, x2i, "B", chainB)
        featv = p.val("V")
        for i in range(2):
            tv = p.emit("tensor", lambda t, i=i: [
                t.transpose(TB[i % 2][:, 128 * dg:128 * (dg + 1)],
                            featbf[:, 1024 * i + 128 * dg:1024 * i + 128 * (dg + 1)],
                            identT[:]) for dg in range(8)][-1],
                wait=[("V", featv), ("V", ev_hist[i % 2])])
            last_fTt = tv
            ev_hist[i % 2] = p.emit("vector", lambda v, i=i: v.tensor_copy(
                TshT[:].rearrange("p (g t) -> p g t", g=8)[:, :, 128 * i:128 * (i + 1)],
                TB[i % 2][:].rearrange("p (g t) -> p g t", g=8)),
                wait=[("T", tv)])
        featTv = p.val("V")
        for i in range(2):
            tden = p.emit("tensor", lambda t, i=i: [
                t.matmul(PBa[i % 2][:, 0:8],
                         TshT[:, 256 * dg + 128 * i:256 * dg + 128 * (i + 1)],
                         KsM[:, 8 * dg:8 * (dg + 1)],
                         start=(dg == 0), stop=(dg == 7)) for dg in range(8)][-1],
                wait=[("V", featTv), ("V", kv_ready)] + list(qfree[(0, i)]))
            p.emit("vector", lambda v, i=i: v.tensor_scalar(
                Zb[:, 8 * i:8 * (i + 1)], PBa[i % 2][:, 0:8], AEPS, None, ALU.add),
                wait=[("T", tden)])
            zv = p.emit("vector", lambda v, i=i: v.reciprocal(
                Zb[:, 8 * i:8 * (i + 1)], Zb[:, 8 * i:8 * (i + 1)]))
            tmsg = p.emit("tensor", lambda t, i=i: [
                t.matmul(PBb[0 if h < 4 else 1][:, 128 * (h % 4):128 * (h % 4) + 128],
                         TshT[:, 256 * h + 128 * i:256 * h + 128 * (i + 1)],
                         KVbf[:, 128 * h:128 * (h + 1)],
                         start=(h % 4 == 0), stop=(h % 4 == 3),
                         skip_group_check=True) for h in range(8)][-1],
                wait=[("V", zv), ("V", kv_ready), ("S", last_mo_s)]
                     + list(qfree[(1, 0)]) + list(qfree[(1, 1)]))
            for h in range(8):
                sv = p.emit("scalar", lambda s, i=i, h=h: s.activation(
                    out=msgsbf[:, 1024 * i + 128 * h:1024 * i + 128 * (h + 1)],
                    in_=PBb[0 if h < 4 else 1][:, 128 * (h % 4):128 * (h % 4) + 128],
                    func=AF.Identity, scale=Zb[:, 8 * i + h:8 * i + h + 1]),
                    wait=[("T", tmsg)])
            last_mo_s = p.val("S")
        msgv = p.val("S")
        for i in range(2):
            tv = p.emit("tensor", lambda t, i=i: [
                t.transpose(TB[i % 2][:, 128 * dg:128 * (dg + 1)],
                            msgsbf[:, 1024 * i + 128 * dg:1024 * i + 128 * (dg + 1)],
                            identT[:]) for dg in range(8)][-1],
                wait=[("S", msgv), ("V", ev_hist[i % 2])])
            ev_hist[i % 2] = p.emit("vector", lambda v, i=i: v.tensor_copy(
                TshT[:].rearrange("p (g t) -> p g t", g=8)[:, :, 128 * i:128 * (i + 1)],
                TB[i % 2][:].rearrange("p (g t) -> p g t", g=8)),
                wait=[("T", tv)])
        msgsTv = p.val("V")
        for i in range(2):
            tmo = p.emit("tensor", lambda t, i=i: [
                t.matmul((PBa if i % 2 == 0 else PBb)[f][:, :],
                         TshT[:, 256 * cg + 128 * i:256 * cg + 128 * (i + 1)],
                         Wm[:, 1024 * cg + 512 * f:1024 * cg + 512 * (f + 1)],
                         start=(cg == 0), stop=(cg == 7))
                for f in range(2) for cg in range(8)][-1],
                wait=[("V", msgsTv), ("S", last_mo_s)])
            vmo = p.emit("vector", lambda v, i=i: [
                v.tensor_copy(scrF[:, 512 * f:512 * (f + 1)],
                              (PBa if i % 2 == 0 else PBb)[f][:, :])
                for f in range(2)][-1], wait=[("T", tmo), ("S", p.val("S"))])
            sF = emit_ln(p, [scrF[:]], [scr1[:]], stats, 10, scr2, [("V", vmo)])
            xr = p.emit("sync", lambda s, r0=r0, i=i: s.dma_start(
                out=xrel[:], in_=x2i[r0 + 128 * i:r0 + 128 * (i + 1), :]),
                wait=[("V", last_vx3)], dma="dx2C")
            p.emit("vector", lambda v, i=i: v.tensor_mul(scr1[:], scr1[:], gbB["g1"][:]),
                   wait=[("S", sF)])
            p.emit("vector", lambda v: v.tensor_add(scr1[:], scr1[:], gbB["b1"][:]))
            guardsC["ln2x"] = p.emit("vector", lambda v, i=i: v.tensor_add(
                scr1[:], scr1[:], xblk[:, 1024 * i:1024 * (i + 1)]))
            vx3 = p.emit("vector", lambda v, i=i: v.tensor_add(
                x3b[:, 1024 * i:1024 * (i + 1)], scr1[:], xrel[:]),
                wait=[("dx2C", xr), ("dsoC", st_prev)])
            last_vx3 = vx3
            p.emit("vector", lambda v, i=i: v.tensor_copy(
                msgobf[:, 1024 * i:1024 * (i + 1)], scrF[:]),
                wait=[("T", last_moT)])
        movt = p.val("V")
        for i in range(2):
            tv = p.emit("tensor", lambda t, i=i: [
                t.transpose(TB[i % 2][:, 128 * dg:128 * (dg + 1)],
                            msgobf[:, 1024 * i + 128 * dg:1024 * i + 128 * (dg + 1)],
                            identT[:]) for dg in range(8)][-1],
                wait=[("V", movt), ("V", ev_hist[i % 2])])
            last_moT = tv
            ev_hist[i % 2] = p.emit("vector", lambda v, i=i: v.tensor_copy(
                TshT[:].rearrange("p (g t) -> p g t", g=8)[:, :, 128 * i:128 * (i + 1)],
                TB[i % 2][:].rearrange("p (g t) -> p g t", g=8)),
                wait=[("T", tv)])
        moTv = p.val("V")
        hfree = [0, 0]
        for fs in range(16):
            hb = PBa[fs % 2]
            tv = p.emit("tensor", lambda t, fs=fs, hb=hb: [
                t.matmul(hb[:, 0:256],
                         W1[:, 2048 * cg + 128 * fs:2048 * cg + 128 * (fs + 1)],
                         TshT[:, 256 * cg:256 * (cg + 1)],
                         start=(cg == 0), stop=(cg == 7)) for cg in range(8)][-1],
                wait=[("V", moTv), ("S", hfree[fs % 2]), ("S", last_mo_s if fs < 2 else 0)])
            hfree[fs % 2] = p.emit("scalar", lambda s, fs=fs, hb=hb: s.activation(
                out=h1T[:, 256 * fs:256 * (fs + 1)], in_=hb[:, 0:256], func=AF.Relu),
                wait=[("T", tv)])
        h1v = p.val("S")
        for i in range(2):
            tml = p.emit("tensor", lambda t, i=i: [
                t.matmul((PBb if i % 2 == 0 else PBa)[dh][:, :],
                         h1T[:, 256 * fs + 128 * i:256 * fs + 128 * (i + 1)],
                         W2[:, 1024 * fs + 512 * dh:1024 * fs + 512 * (dh + 1)],
                         start=(fs == 0), stop=(fs == 15))
                for dh in range(2) for fs in range(16)][-1],
                wait=[("S", h1v), ("S", last_mo_s)])
            vml = p.emit("vector", lambda v, i=i: [
                v.tensor_copy(scrF[:, 512 * dh:512 * (dh + 1)],
                              (PBb if i % 2 == 0 else PBa)[dh][:, :])
                for dh in range(2)][-1], wait=[("T", tml)])
            sF = emit_ln(p, [scrF[:]], [scr1[:]], stats, 10, scr2, [("V", vml)])
            p.emit("vector", lambda v: v.tensor_mul(scr1[:], scr1[:], gbB["g2"][:]),
                   wait=[("S", sF)])
            p.emit("vector", lambda v: v.tensor_add(scr1[:], scr1[:], gbB["b2"][:]))
            vx3 = p.emit("vector", lambda v, i=i: v.tensor_add(
                x3b[:, 1024 * i:1024 * (i + 1)], x3b[:, 1024 * i:1024 * (i + 1)], scr1[:]))
            guardsC["scr1"] = vx3
            st_prev = p.emit("gpsimd", lambda g, r0=r0, i=i: g.dma_start(
                out=x3o[r0 + 128 * i:r0 + 128 * (i + 1), :],
                in_=x3b[:, 1024 * i:1024 * (i + 1)]),
                wait=[("V", vx3)], dma="dsoC")
        last_mo_s = p.val("S")
        for kk in qfree:
            qfree[kk] = [("V", p.val("V")), ("S", p.val("S"))]
    p.emit("gpsimd", lambda g: g.nop(), wait=[("dsoC", p.val("dsoC"))])



FP4_THR = [0.25, 0.75, 1.25, 1.75, 2.5, 3.5, 5.0]


def AP0(ap, k):
    from concourse.ap import AP as _AP
    dims = [list(d) for d in ap.ap] + [[0, k]]
    return _AP(tensor=ap.tensor, offset=ap.offset, ap=dims)


def _apslice(ap, cs):
    return ap[:, cs]


def build_stageD(p, es, T, x3i, x4o):
    nc = p.nc
    I = 2816
    wg = nc.dram_tensor("L1_ffn_w_gate", [I, D], F32, kind="ExternalInput").ap()
    wu = nc.dram_tensor("L1_ffn_w_up", [I, D], F32, kind="ExternalInput").ap()
    wd = nc.dram_tensor("L1_ffn_w_down", [D, I], F32, kind="ExternalInput").ap()
    NB = T // 256
    sb = lambda name, shape, dt=F32: es.enter_context(nc.sbuf_tensor(name, shape, dt))

    identT = sb("D_identT", [128, 128], BF16)

    def c_memsets(g):
        return g.memset(identT[:], 0.0)

    def c_fills(g):
        return g.affine_select(out=identT[:], in_=identT[:], compare_op=ALU.not_equal,
                               fill=1.0, base=0, pattern=[[-1, 128]], channel_multiplier=1)

    cm = p.emit("gpsimd", c_memsets)
    cstv = p.emit("gpsimd", c_fills, wait=[("G", cm)])
    cstv = p.reg_floats(es, [EPS, -0.5], "D_cstf") or cstv

    PA = [es.enter_context(nc.psum_tensor(f"D_pa{i}", [128, 512], F32)) for i in range(2)]
    PBp = [es.enter_context(nc.psum_tensor(f"D_pb{i}", [128, 512], F32)) for i in range(2)]
    PD = [es.enter_context(nc.psum_tensor(f"D_pd{i}", [128, 512], F32)) for i in range(2)]
    TB = [es.enter_context(nc.psum_tensor(f"D_tb{i}", [128, 1024], BF16)) for i in range(2)]

    wgT = sb("D_wgT", [128, 8 * I], BF16)
    wuT = sb("D_wuT", [128, 8 * I], BF16)
    wdT = sb("D_wdT", [128, 22 * 1024], BF16)
    wstgD = sb("D_wstg", [128, I])
    wqbf = sb("D_wqbf", [128, I], BF16)
    scrA = sb("D_scrA", [128, I])
    scrQ = sb("D_scrQ", [128, I])
    scrM = sb("D_scrM", [128, 2 * (I // 16)])
    scrS = sb("D_scrS", [128, I], BF16)
    xblk = sb("D_xblk", [128, 1024])
    xqbf = sb("D_xqbf", [128, 2 * 1024], BF16)
    xqT = sb("D_xqT", [128, 8 * 256], BF16)
    hbuf = [wstgD, sb("D_hbuf1", [128, I], BF16)]
    hqbf = wqbf
    hqT = sb("D_hqT", [128, 22 * 128], BF16)
    xrel = sb("D_xrel", [128, 1024])
    sil = sb("D_sil", [128, 512])
    x4t = xrel
    stats = sb("D_stats", [128, 16])

    gq = {"v": 0, "s": 0, "g": 0}

    def emit_qdq(src_ap, sgn_src, out_ap, W, waits, out_guard=0, pre=None):
        NBK16 = W // 16
        BV = NBK16
        if pre is None:
            sv_abs = p.emit("scalar", lambda s: s.activation(
                out=scrA[:, 0:W], in_=src_ap, func=AF.Abs),
                wait=list(waits) + [("V", gq["v"]), ("G", gq["g"])])
            ss = p.emit("scalar", lambda s: s.sign(out=scrS[:, 0:W], in_=sgn_src))
        else:
            sv_abs, ss = pre
        def _half(eng, b0, b1):
            nb = b1 - b0
            cs = slice(16 * b0, 16 * b1)
            sA = scrA[:, cs]
            sQ = scrQ[:, cs]
            sS = scrS[:, cs]
            a3 = lambda ap: ap.rearrange("p (b k) -> p b k", k=16)
            W2 = 16 * nb
            o1, o2, o3 = 16 * b0, 16 * b0 + W2 // 2, 16 * b0 + W2 // 2 + W2 // 4
            o4 = o3 + W2 // 8
            m0 = slice(b0, b1)
            m1 = slice(NBK16 + b0, NBK16 + b1)
            E = lambda fn, w=(): p.emit(eng, fn, wait=w)
            p.emit("vector", lambda e: e.tensor_tensor(
                scrQ[:, o1:o1 + W2 // 2].rearrange("p (b k) -> p b k", k=8),
                a3(sA)[:, :, 0:8], a3(sA)[:, :, 8:16], ALU.max),
                wait=[("S", sv_abs)])
            p.emit("vector", lambda e: e.tensor_tensor(
                scrQ[:, o2:o2 + W2 // 4].rearrange("p (b k) -> p b k", k=4),
                scrQ[:, o1:o1 + W2 // 2].rearrange("p (b k) -> p b k", k=8)[:, :, 0:4],
                scrQ[:, o1:o1 + W2 // 2].rearrange("p (b k) -> p b k", k=8)[:, :, 4:8],
                ALU.max))
            p.emit("vector", lambda e: e.tensor_tensor(
                scrQ[:, o3:o3 + W2 // 8].rearrange("p (b k) -> p b k", k=2),
                scrQ[:, o2:o2 + W2 // 4].rearrange("p (b k) -> p b k", k=4)[:, :, 0:2],
                scrQ[:, o2:o2 + W2 // 4].rearrange("p (b k) -> p b k", k=4)[:, :, 2:4],
                ALU.max))
            p.emit("vector", lambda e: e.tensor_tensor(
                scrM[:, m0],
                scrQ[:, o3:o3 + W2 // 8].rearrange("p (b k) -> p b k", k=2)[:, :, 0:1]
                .rearrange("p b k -> p (b k)"),
                scrQ[:, o3:o3 + W2 // 8].rearrange("p (b k) -> p b k", k=2)[:, :, 1:2]
                .rearrange("p b k -> p (b k)"), ALU.max))
            p.emit("vector", lambda e: e.tensor_scalar(
                scrM[:, m0], scrM[:, m0], 1.0 / 6.0, 1e-12, ALU.mult, ALU.max))
            rv = p.emit("vector", lambda e: e.reciprocal(scrM[:, m1], scrM[:, m0]))
            E(lambda e: e.tensor_tensor(a3(sA), a3(sA), AP0(scrM[:, m1], 16), ALU.mult),
              w=[("V", rv)])
            if eng == "vector":
                E(lambda e: e.tensor_scalar(sQ[:], sA[:], FP4_THR[0], None, ALU.is_gt))
                for c in FP4_THR[1:4]:
                    E(lambda e, c=c: e.scalar_tensor_tensor(sQ[:], sA[:], c, sQ[:],
                                                            ALU.is_gt, ALU.add))
                E(lambda e: e.tensor_scalar_mul(sQ[:], sQ[:], 0.5))
                for c in FP4_THR[4:6]:
                    E(lambda e, c=c: e.scalar_tensor_tensor(sQ[:], sA[:], c, sQ[:],
                                                            ALU.is_gt, ALU.add))
                for _ in range(2):
                    E(lambda e: e.scalar_tensor_tensor(sQ[:], sA[:], FP4_THR[6], sQ[:],
                                                       ALU.is_gt, ALU.add))
            else:
                tmp = _apslice(out_ap, cs)
                E(lambda e: e.tensor_scalar(sQ[:], sA[:], FP4_THR[0], None, ALU.is_gt),
                  w=[("T", out_guard)])
                for c in FP4_THR[1:4]:
                    E(lambda e, c=c: e.tensor_scalar(tmp, sA[:], c, None, ALU.is_gt))
                    E(lambda e: e.tensor_tensor(sQ[:], sQ[:], tmp, ALU.add))
                E(lambda e: e.tensor_scalar_mul(sQ[:], sQ[:], 0.5))
                for c in FP4_THR[4:6]:
                    E(lambda e, c=c: e.tensor_scalar(tmp, sA[:], c, None, ALU.is_gt))
                    E(lambda e: e.tensor_tensor(sQ[:], sQ[:], tmp, ALU.add))
                E(lambda e: e.tensor_scalar(tmp, sA[:], FP4_THR[6], None, ALU.is_gt))
                E(lambda e: e.tensor_tensor(sQ[:], sQ[:], tmp, ALU.add))
                E(lambda e: e.tensor_tensor(sQ[:], sQ[:], tmp, ALU.add))
            E(lambda e: e.tensor_tensor(a3(sQ), a3(sQ), AP0(scrM[:, m0], 16), ALU.mult))
            return E(lambda e, cs=cs: e.tensor_mul(
                _apslice(out_ap, cs), scrQ[:, cs], sS[:]),
                w=[("S", ss), ("T", out_guard)])

        fvs = [_half("vector", 0, BV)]
        fvs.append(_half("gpsimd", BV, NBK16) if BV < NBK16 else 0)
        gq["v"] = fvs[0]
        gq["g"] = fvs[1]
        gq["s"] = ss
        return fvs[0], fvs[1]

    wld_prev = 0
    ev_hist = [0, 0]
    for wsrc, wdst in ((wg, wgT), (wu, wuT)):
        for ch in range(22):
            ld = p.emit("sync", lambda s, wsrc=wsrc, ch=ch: s.dma_start(
                out=wstgD[:, 0:1024], in_=wsrc[128 * ch:128 * (ch + 1), :]),
                wait=[("S", gq["s"]), ("V", gq["v"]), ("G", gq["g"]),
                      ("T", wld_prev)], dma="dwD")
            fv, fg = emit_qdq(wstgD[:, 0:1024], wstgD[:, 0:1024], wqbf[:, 0:1024], 1024,
                          [("dwD", ld), ("G", cstv)], out_guard=wld_prev)
            for tb in range(8):
                tv = p.emit("tensor", lambda t, tb=tb: t.transpose(
                    TB[tb % 2][:, 128 * (tb // 2):128 * (tb // 2) + 128],
                    wqbf[:, 128 * tb:128 * (tb + 1)], identT[:]),
                    wait=[("V", fv), ("G", fg), ("V", ev_hist[tb % 2]), ("G", cstv)])
                wld_prev = tv
                ev_hist[tb % 2] = p.emit("vector", lambda v, tb=tb, ch=ch, wdst=wdst: v.tensor_copy(
                    wdst[:, I * tb + 128 * ch:I * tb + 128 * (ch + 1)],
                    TB[tb % 2][:, 128 * (tb // 2):128 * (tb // 2) + 128]),
                    wait=[("T", tv)])
    for ch in range(8):
        ld = p.emit("sync", lambda s, ch=ch: s.dma_start(
            out=wstgD[:], in_=wd[128 * ch:128 * (ch + 1), :]),
            wait=[("S", gq["s"]), ("V", gq["v"]), ("T", wld_prev)], dma="dwD")
        fv, fg = emit_qdq(wstgD[:], wstgD[:], wqbf[:], I, [("dwD", ld), ("G", cstv)],
                      out_guard=wld_prev)
        for tb in range(22):
            tv = p.emit("tensor", lambda t, tb=tb: t.transpose(
                TB[tb % 2][:, 128 * ((tb // 2) % 4):128 * ((tb // 2) % 4) + 128],
                wqbf[:, 128 * tb:128 * (tb + 1)], identT[:]),
                wait=[("V", fv), ("G", fg), ("V", ev_hist[tb % 2]), ("G", cstv)])
            wld_prev = tv
            ev_hist[tb % 2] = p.emit("vector", lambda v, tb=tb, ch=ch: v.tensor_copy(
                wdT[:, 1024 * tb + 128 * ch:1024 * tb + 128 * (ch + 1)],
                TB[tb % 2][:, 128 * ((tb // 2) % 4):128 * ((tb // 2) % 4) + 128]),
                wait=[("T", tv)])
    wq_done_v = p.val("V")
    wq_done_t = p.val("T")

    ln_s_prev = 0
    last_xq_t = 0
    last_hq_t = wq_done_t
    last_x4 = 0
    st_prev = 0
    gu_free = [0, 0]
    dn_free = [0, 0]
    fr_ch = [(0, 512), (512, 1024), (1024, 1536), (1536, 2048), (2048, 2560), (2560, I)]
    for bk in range(NB):
        r0 = bk * 256
        for i in range(2):
            lx = p.emit("sync", lambda s, r0=r0, i=i: s.dma_start(
                out=xblk[:], in_=x3i[r0 + 128 * i:r0 + 128 * (i + 1), :]),
                wait=[("S", ln_s_prev), ("S", gq["s"]), ("V", gq["v"]),
                      ("G", gq["g"])], dma="dxD")
            ln_s_prev = emit_ln(p, [xblk[:]], [xblk[:]], stats, 0, scrA[:, 0:1024],
                                [("dxD", lx), ("G", cstv), ("V", gq["v"]),
                                 ("G", gq["g"]),
                                 ("V", wq_done_v if bk == 0 else 0)])
            fv, fg = emit_qdq(xblk[:], xblk[:], xqbf[:, 1024 * i:1024 * (i + 1)], 1024,
                          [("S", ln_s_prev)], out_guard=last_xq_t)
            tvs = p.emit("tensor", lambda t, i=i: [
                t.transpose(TB[i % 2][:, 128 * dg:128 * (dg + 1)],
                            xqbf[:, 1024 * i + 128 * dg:1024 * i + 128 * (dg + 1)],
                            identT[:]) for dg in range(8)][-1],
                wait=[("V", fv), ("G", fg), ("V", ev_hist[i % 2]),
                      ("T", wq_done_t if bk == 0 and i == 0 else 0)])
            last_xq_t = tvs
            ev_hist[i % 2] = p.emit("vector", lambda v, i=i: v.tensor_copy(
                xqT[:].rearrange("p (g t) -> p g t", g=8)[:, :, 128 * i:128 * (i + 1)],
                TB[i % 2][:].rearrange("p (g t) -> p g t", g=8)),
                wait=[("T", tvs)])
        xqTv = p.val("V")
        hv = [0, 0]
        qres = [None, None]
        for i in range(2):
            hb_i = hbuf[i]
            for ci, (n0, n1) in enumerate(fr_ch):
                nw = n1 - n0
                tg = p.emit("tensor", lambda t, i=i, n0=n0, nw=nw, ci=ci: [
                    t.matmul(PA[ci % 2][:, 0:nw],
                             xqT[:, 256 * cg + 128 * i:256 * cg + 128 * (i + 1)],
                             wgT[:, I * cg + n0:I * cg + n0 + nw],
                             start=(cg == 0), stop=(cg == 7)) for cg in range(8)][-1],
                    wait=[("V", xqTv), ("V", gu_free[ci % 2])])
                tu = p.emit("tensor", lambda t, i=i, n0=n0, nw=nw, ci=ci: [
                    t.matmul(PBp[ci % 2][:, 0:nw],
                             xqT[:, 256 * cg + 128 * i:256 * cg + 128 * (i + 1)],
                             wuT[:, I * cg + n0:I * cg + n0 + nw],
                             start=(cg == 0), stop=(cg == 7)) for cg in range(8)][-1])
                sg = p.emit("scalar", lambda s, n0=n0, nw=nw, ci=ci: s.activation(
                    out=sil[:, 0:nw], in_=PA[ci % 2][:, 0:nw], func=AF.Silu),
                    wait=[("T", tg), ("V", gu_free[(ci + 1) % 2])])
                gu_free[ci % 2] = p.emit("vector", lambda v, n0=n0, nw=nw, ci=ci,
                                         hb_i=hb_i: v.tensor_tensor(
                    hb_i[:, n0:n0 + nw], sil[:, 0:nw],
                    PBp[ci % 2][:, 0:nw], ALU.mult),
                    wait=[("T", tu), ("S", sg), ("S", gq["s"])])
            hv[i] = p.val("V")
            if i == 0:
                pa0 = p.emit("scalar", lambda s: s.activation(
                    out=scrA[:, 0:I], in_=hbuf[0][:], func=AF.Abs),
                    wait=[("V", hv[0]), ("V", gq["v"]), ("G", gq["g"]),
                          ("S", gq["s"])])
                ps0 = p.emit("scalar", lambda s: s.sign(out=scrS[:, 0:I],
                                                        in_=hbuf[0][:]))
                qres[0] = (pa0, ps0)
        for i in range(2):
            fv, fg = emit_qdq(
                hbuf[i][:], hbuf[i][:], hqbf[:], I, [("V", hv[1])],
                out_guard=last_hq_t, pre=qres[0] if i == 0 else None)
            tl = 0
            for tb in range(22):
                tv = p.emit("tensor", lambda t, tb=tb: t.transpose(
                    TB[tb % 2][:, 128 * ((tb // 2) % 4):128 * ((tb // 2) % 4) + 128],
                    hqbf[:, 128 * tb:128 * (tb + 1)], identT[:]),
                    wait=[("V", fv), ("G", fg), ("V", ev_hist[tb % 2])])
                tl = tv
                last_hq_t = tv
                ev_hist[tb % 2] = p.emit("vector", lambda v, tb=tb: v.tensor_copy(
                    hqT[:, 128 * tb:128 * (tb + 1)],
                    TB[tb % 2][:, 128 * ((tb // 2) % 4):128 * ((tb // 2) % 4) + 128]),
                    wait=[("T", tv)])
            hqTv = p.val("V")
            xr = p.emit("sync", lambda s, r0=r0, i=i: s.dma_start(
                out=xrel[:], in_=x3i[r0 + 128 * i:r0 + 128 * (i + 1), :]),
                wait=[("V", last_x4), ("dsoD", st_prev)], dma="dx2D")
            for dh in range(2):
                td = p.emit("tensor", lambda t, dh=dh: [
                    t.matmul(PD[dh][:, :],
                             hqT[:, 128 * ng:128 * (ng + 1)],
                             wdT[:, 1024 * ng + 512 * dh:1024 * ng + 512 * (dh + 1)],
                             start=(ng == 0), stop=(ng == 21)) for ng in range(22)][-1],
                    wait=[("V", hqTv), ("V", dn_free[dh])])
                dn_free[dh] = p.emit("vector", lambda v, dh=dh: v.tensor_tensor(
                    x4t[:, 512 * dh:512 * (dh + 1)], xrel[:, 512 * dh:512 * (dh + 1)],
                    PD[dh][:, :], ALU.add),
                    wait=[("T", td), ("dx2D", xr), ("dsoD", st_prev)])
            last_x4 = p.val("V")
            st_prev = p.emit("gpsimd", lambda g, r0=r0, i=i: g.dma_start(
                out=x4o[r0 + 128 * i:r0 + 128 * (i + 1), :], in_=x4t[:]),
                wait=[("V", last_x4)], dma="dsoD")
    p.emit("gpsimd", lambda g: g.nop(), wait=[("dsoD", p.val("dsoD"))])



def _build_A(T):
    nc = bass.Bass("TRN2", target_bir_lowering=False)
    es = ExitStack()
    p = P(nc)
    xin = nc.dram_tensor("x_in", [T, D], F32, kind="ExternalInput").ap()
    x1o = nc.dram_tensor("x1", [T, D], F32, kind="ExternalOutput").ap()
    with es:
        build_stageA(p, es, T, xin, x1o)
        p.build("stageA")
    return nc


def _build_B(T):
    nc = bass.Bass("TRN2", target_bir_lowering=False)
    es = ExitStack()
    p = P(nc)
    x1i = nc.dram_tensor("x1_in", [T, D], F32, kind="ExternalInput").ap()
    x2o = nc.dram_tensor("x2", [T, D], F32, kind="ExternalOutput").ap()
    with es:
        build_stageB(p, es, T, x1i, x2o)
        p.build("stageB")
    return nc


def _build_C(T):
    nc = bass.Bass("TRN2", target_bir_lowering=False)
    es = ExitStack()
    p = P(nc)
    x2i = nc.dram_tensor("x2_in", [T, D], F32, kind="ExternalInput").ap()
    x3o = nc.dram_tensor("x3", [T, D], F32, kind="ExternalOutput").ap()
    with es:
        build_stageC(p, es, T, x2i, x3o)
        p.build("stageC")
    return nc


def _build_D(T):
    nc = bass.Bass("TRN2", target_bir_lowering=False)
    es = ExitStack()
    p = P(nc)
    x3i = nc.dram_tensor("x3_in", [T, D], F32, kind="ExternalInput").ap()
    x4o = nc.dram_tensor("x4", [T, D], F32, kind="ExternalOutput").ap()
    with es:
        build_stageD(p, es, T, x3i, x4o)
        p.build("stageD")
    return nc


A_KEYS = ["L0_attn_Wx", "L0_attn_Ww", "L0_attn_Wk", "L0_attn_Wv", "L0_attn_Wr",
          "L0_attn_Wo", "L0_attn_ln_g", "L0_attn_ln_b", "L0_attn_bw"]
B_KEYS = ["L0_ffn_keys", "L0_ffn_rw", "L0_ffn_rb", "L0_ffn_w1", "L0_ffn_b1",
          "L0_ffn_w2", "L0_ffn_b2"]
C_KEYS = ["L1_attn_Wq", "L1_attn_Wm", "L1_attn_W1", "L1_attn_W2",
          "L1_attn_g1", "L1_attn_b1", "L1_attn_g2", "L1_attn_b2"]
D_KEYS = ["L1_ffn_w_gate", "L1_ffn_w_up", "L1_ffn_w_down"]


def _build_fused(T):
    nc = bass.Bass("TRN2", target_bir_lowering=False)
    p = P(nc)
    ges = ExitStack()
    p.const_es = ges
    p.reg_floats(ges, [EPS, -0.5, -S0], "cstfG")
    xin = nc.dram_tensor("x_in", [T, D], F32, kind="ExternalInput").ap()
    x1 = nc.dram_tensor("x1_s", [T, D], F32, kind="Internal").ap()
    x2 = nc.dram_tensor("x2_s", [T, D], F32, kind="Internal").ap()
    x3 = nc.dram_tensor("x3_s", [T, D], F32, kind="Internal").ap()
    x4o = nc.dram_tensor("x4", [T, D], F32, kind="ExternalOutput").ap()
    with ExitStack() as esA:
        build_stageA(p, esA, T, xin, x1)
    p.barrier_all()
    with ExitStack() as esB:
        build_stageB(p, esB, T, x1, x2)
    p.barrier_all()
    with ExitStack() as esC:
        build_stageC(p, esC, T, x2, x3)
    p.barrier_all()
    with ExitStack() as esD:
        build_stageD(p, esD, T, x3, x4o)
    p.build("fused")
    ges.close()
    return nc


ALL_KEYS = A_KEYS + B_KEYS + C_KEYS + D_KEYS


def kernel(**inputs):
    T = inputs["L0_attn_x"].shape[1]
    nc = _build_fused(T)
    in_maps = []
    for c in range(8):
        m = {k: np.ascontiguousarray(inputs[k]) for k in ALL_KEYS}
        m["x_in"] = np.ascontiguousarray(inputs["L0_attn_x"][c])
        in_maps.append(m)
    res = run_bass_kernel_spmd(nc, in_maps, core_ids=list(range(8)))
    return np.stack([r["x4"] for r in res.results], axis=0)


def kernel_unfused(**inputs):
    T = inputs["L0_attn_x"].shape[1]

    def run(nc, keys, xkey, xs):
        in_maps = []
        for c in range(8):
            m = {k: np.ascontiguousarray(inputs[k]) for k in keys}
            m[xkey] = np.ascontiguousarray(xs[c])
            in_maps.append(m)
        return run_bass_kernel_spmd(nc, in_maps, core_ids=list(range(8)))

    xs = [inputs["L0_attn_x"][c] for c in range(8)]
    r = run(_build_A(T), A_KEYS, "x_in", xs)
    xs = [q["x1"] for q in r.results]
    r = run(_build_B(T), B_KEYS, "x1_in", xs)
    xs = [q["x2"] for q in r.results]
    r = run(_build_C(T), C_KEYS, "x2_in", xs)
    xs = [q["x3"] for q in r.results]
    r = run(_build_D(T), D_KEYS, "x3_in", xs)
    return np.stack([q["x4"] for q in r.results], axis=0)


# version 14
# speedup vs baseline: 1.1113x; 1.1113x over previous
import numpy as np
from contextlib import ExitStack
import concourse.bass as bass
import concourse.mybir as mybir
from concourse.bass_utils import run_bass_kernel_spmd

F32 = mybir.dt.float32
BF16 = mybir.dt.bfloat16
AF = mybir.ActivationFunctionType
ALU = mybir.AluOpType
AX = mybir.AxisListType.X

D = 1024
PHASE = 99
S0 = 26.0
E2S0 = float(np.exp(2 * S0))
EPS = 1e-5


class P:
    ENGINES = ("sync", "gpsimd", "scalar", "vector", "tensor")
    DMA_SEM = {"sync": "Yd", "gpsimd": "Gd", "scalar": "Sd"}
    PROG_SEM = {"sync": "Y", "gpsimd": "G", "scalar": "S", "vector": "V", "tensor": "T"}

    def __init__(self, nc):
        self.nc = nc
        self.ev = {e: [] for e in self.ENGINES}
        self.semh = {}
        self.semc = {}

    def _sem(self, name):
        if name not in self.semh:
            self.semh[name] = self.nc.alloc_semaphore(name)
            self.semc[name] = 0
        return name

    def val(self, name):
        return self.semc.get(name, 0)

    def emit(self, eng, fn, wait=(), dma=False):
        if dma:
            sem = dma if isinstance(dma, str) else self.DMA_SEM[eng]
            self._sem(sem)
            self.semc[sem] += 16
            v = self.semc[sem]
            wait = list(wait) + [(sem, v - 16)]
        else:
            sem = self.PROG_SEM[eng]
            self._sem(sem)
            prev = self.semc[sem]
            self.semc[sem] += 1
            v = self.semc[sem]
            wait = list(wait) + [(sem, prev)]
        w = {}
        for s, x in wait:
            if x and x > 0:
                self._sem(s)
                w[s] = max(w.get(s, 0), x)
        self.ev[eng].append((fn, tuple(w.items()), sem, 16 if dma else 1))
        return v

    def build(self, name=None):
        with self.nc.Block(name) as block:
            for eng in self.ENGINES:
                evs = self.ev[eng]
                if not evs:
                    continue

                def mk(evs):
                    def body(e):
                        for fn, waits, sem, by in evs:
                            for s, v in waits:
                                e.wait_ge(self.semh[s], v)
                            ins = fn(e)
                            assert ins is not None, "event returned no instruction"
                            ins.then_inc(self.semh[sem], by)
                    return body

                getattr(block, eng)(mk(evs))
        self.ev = {e: [] for e in self.ENGINES}

    def reg_floats(self, es, values, tag="cstf"):
        vals = [v for v in dict.fromkeys(values) if v not in getattr(self, "_regd", set())]
        if not hasattr(self, "_regd"):
            self._regd = set()
        if not vals:
            return 0
        es = getattr(self, "const_es", None) or es
        buf = es.enter_context(self.nc.sbuf_tensor(tag, [128, len(vals)], F32))

        def f(g):
            ins = None
            for i, v in enumerate(vals):
                ins = g.memset(buf[:, i:i + 1], float(v))
            return ins

        ev = self.emit("gpsimd", f)
        for i, v in enumerate(vals):
            self.nc.const_aps.register(F32, float(v), buf[:, i:i + 1])
            self._regd.add(v)
        return ev

    def snap(self):
        return [(s, self.val(s)) for s, v in self.semc.items() if v > 0]

    def barrier_all(self):
        w = self.snap()
        for eng in self.ENGINES:
            self.emit(eng, lambda e: e.nop(), wait=w)


def emit_ln(p, src_tiles, dst_tiles, stats, col0, sqscr, wait_src):
    nt = len(src_tiles)
    sl = lambda k: stats[:, col0 + k * nt: col0 + (k + 1) * nt]
    s1, s2, mu, rs, nb = (sl(k) for k in range(5))
    for i, t in enumerate(src_tiles):
        p.emit("vector", lambda v, i=i, t=t: v.reduce_sum(s1[:, i:i + 1], t, AX),
               wait=wait_src)
        p.emit("scalar", lambda s, i=i, t=t: s.activation(
            out=sqscr[:], in_=t, func=AF.Square, accum_out=s2[:, i:i + 1]),
            wait=wait_src)
    vv, sv = p.val("V"), p.val("S")
    p.emit("vector", lambda v: v.tensor_scalar_mul(mu[:], s1[:], 1.0 / D), wait=[("V", vv)])
    p.emit("vector", lambda v: v.tensor_scalar_mul(rs[:], s2[:], 1.0 / D), wait=[("S", sv)])
    p.emit("vector", lambda v: v.tensor_mul(nb[:], mu[:], mu[:]))
    p.emit("vector", lambda v: v.tensor_tensor(rs[:], rs[:], nb[:], ALU.subtract))
    vv = p.val("V")
    p.emit("scalar", lambda s: s.activation(out=rs[:], in_=rs[:], func=AF.Ln, bias=EPS),
           wait=[("V", vv)])
    p.emit("scalar", lambda s: s.activation(out=rs[:], in_=rs[:], func=AF.Exp, scale=-0.5))
    sv = p.val("S")
    p.emit("vector", lambda v: v.scalar_tensor_tensor(nb[:], mu[:], -1.0, rs[:],
                                                      ALU.mult, ALU.mult),
           wait=[("S", sv)])
    vv = p.val("V")
    for i, (srct, dstt) in enumerate(zip(src_tiles, dst_tiles)):
        p.emit("scalar", lambda s, i=i, srct=srct, dstt=dstt: s.activation(
            out=dstt, in_=srct, func=AF.Identity,
            scale=rs[:, i:i + 1], bias=nb[:, i:i + 1]),
            wait=[("V", vv)])
    return p.val("S")



def build_stageA(p, es, T, xin, x1o):
    nc = p.nc
    wts = {k: nc.dram_tensor(k, [D, D], F32, kind="ExternalInput").ap()
           for k in ("L0_attn_Wx", "L0_attn_Ww", "L0_attn_Wk", "L0_attn_Wv",
                     "L0_attn_Wr", "L0_attn_Wo")}
    vecs = {k: nc.dram_tensor(k, [D], F32, kind="ExternalInput").ap()
            for k in ("L0_attn_ln_g", "L0_attn_ln_b", "L0_attn_bw")}
    NB = T // 512
    sb = lambda name, shape, dt=F32: es.enter_context(nc.sbuf_tensor(name, shape, dt))

    identT = sb("identT", [128, 128], BF16)
    U64 = sb("U64", [128, 128], F32)
    Bsel = sb("Bsel", [128, 128], F32)
    cind = sb("cind", [128, 2], F32)
    maskE = sb("maskE", [128, 128], F32)
    ones1 = sb("ones1", [1, 128], F32)

    def c_memsets(g):
        g.memset(identT[:], 0.0)
        g.memset(U64[:], 0.0)
        g.memset(Bsel[:], 0.0)
        g.memset(cind[:], 0.0)
        g.memset(maskE[:], 0.0)
        return g.memset(ones1[:], 1.0)

    def c_fills(g):
        g.affine_select(out=identT[:], in_=identT[:], compare_op=ALU.not_equal,
                        fill=1.0, base=0, pattern=[[-1, 128]], channel_multiplier=1)
        for c in range(2):
            blk = U64[64 * c:64 * c + 64, 64 * c:64 * c + 64]
            g.affine_select(out=blk, in_=blk, compare_op=ALU.is_gt,
                            fill=1.0, base=0, pattern=[[-1, 64]], channel_multiplier=1)
        g.affine_select(out=Bsel[:, 0:64], in_=Bsel[:, 0:64], compare_op=ALU.not_equal,
                        fill=0.5, base=-63, pattern=[[0, 64]], channel_multiplier=1)
        g.affine_select(out=Bsel[:, 64:128], in_=Bsel[:, 64:128], compare_op=ALU.not_equal,
                        fill=0.5, base=-127, pattern=[[0, 64]], channel_multiplier=1)
        g.affine_select(out=cind[:, 0:1], in_=cind[:, 0:1], compare_op=ALU.is_gt,
                        fill=1.0, base=-63, pattern=[[0, 1]], channel_multiplier=1)
        g.affine_select(out=cind[:, 1:2], in_=cind[:, 1:2], compare_op=ALU.is_ge,
                        fill=1.0, base=63, pattern=[[0, 1]], channel_multiplier=-1)
        ins = None
        for c in range(2):
            blk = maskE[64 * c:64 * c + 64, 64 * c:64 * c + 64]
            ins = g.affine_select(out=blk, in_=blk, compare_op=ALU.is_gt,
                                  fill=E2S0, base=0, pattern=[[-1, 64]], channel_multiplier=1)
        return ins

    def _stop(n, buf):
        if PHASE > n:
            return False
        vv = p.snap()
        p.emit("gpsimd", lambda g, buf=buf: g.dma_start(
            out=x1o[0:128, 0:buf.shape[1] if buf.shape[1] <= 1024 else 1024],
            in_=buf[:, 0:1024] if buf.shape[1] >= 1024 else buf[:]),
               wait=vv, dma="dso")
        p.emit("gpsimd", lambda g: g.nop(), wait=[("dso", p.val("dso"))])
        return True

    cm = p.emit("gpsimd", c_memsets)
    cstv = p.emit("gpsimd", c_fills, wait=[("G", cm)])
    cstv = p.reg_floats(es, [EPS, -0.5, -S0], "cstfA") or cstv

    W = {k: sb(k + "_sb", [128, 8 * 1024], BF16) for k in wts}
    row = sb("rowA", [1, 1024], F32)
    gB, bB, bwB = sb("gB", [128, 1024]), sb("bB", [128, 1024]), sb("bwB", [128, 1024])
    for k in wts:
        p.emit("gpsimd", lambda g, k=k: g.dma_start(
            out=W[k][:].rearrange("p (g d) -> p g d", g=8),
            in_=wts[k].rearrange("(g p) d -> p g d", p=128)), dma="dw")
    wldG = p.val("dw")

    xblk = sb("xblk", [128, 4 * 1024])
    stats = sb("statsA", [128, 48])
    xnbf = sb("xnbf", [128, 4 * 1024], BF16)
    xnT = sb("xnT", [128, 8 * 512], BF16)
    z1T = sb("z1T", [128, 8 * 512], BF16)
    labuf = sb("labuf", [128, 1024])
    escrA = sb("escrA", [128, 1024])
    escrB = sb("escrB", [128, 1024])
    escrC = sb("escrC", [128, 1024])
    escrR = sb("escrR", [128, 1024])
    Lsb = sb("LsbA", [128, 1024])
    Rqbf = sb("Rqbf", [128, 1024], BF16)
    Kdbf = sb("Kdbf", [128, 1024], BF16)
    Kec0 = sb("Kec0", [128, 1024], BF16)
    Kec1 = sb("Kec1", [128, 1024], BF16)
    Rxbf = sb("Rxbf", [128, 1024], BF16)
    RqT = sb("RqT", [128, 1024], BF16)
    KdT = sb("KdT", [128, 1024], BF16)
    RxT = sb("RxT", [128, 1024], BF16)
    AC = sb("ACA", [128, 16], F32)
    PTm = [sb(f"PTm{i}", [128, 128], BF16) for i in range(2)]
    S2 = sb("S2A", [128, 512], F32)
    Sblk = sb("SblkA", [128, 8 * 128], BF16)
    vbf = sb("vbfA", [128, 1024], BF16)
    x1b1 = sb("x1b", [128, 1024])
    x1b = [x1b1, x1b1]
    xrel = escrC
    outs = Kdbf
    outsT = RqT

    PB = [es.enter_context(nc.psum_tensor(f"pbA{i}", [128, 512], F32)) for i in range(6)]
    TB = [es.enter_context(nc.psum_tensor(f"tbA{i}", [128, 1024], BF16)) for i in range(2)]

    p.emit("vector", lambda v: v.memset(S2[:], 0.0))
    p.emit("vector", lambda v: v.memset(Sblk[:], 0.0))
    p.emit("vector", lambda v: v.memset(Kec0[:], 0.0))
    p.emit("vector", lambda v: v.memset(Kec1[:], 0.0))
    last_bc_v = 0
    for key, dst in (("L0_attn_ln_g", gB), ("L0_attn_ln_b", bB), ("L0_attn_bw", bwB)):
        ld = p.emit("sync", lambda s, key=key: s.dma_start(out=row[:], in_=vecs[key][None, :]),
                    wait=[("V", last_bc_v)], dma="drow")
        for half in range(2):
            tv = p.emit("tensor", lambda t, half=half: t.matmul(
                PB[0][:, :], ones1[:], row[:, 512 * half:512 * (half + 1)],
                start=True, stop=True),
                wait=[("drow", ld), ("G", cstv), ("V", last_bc_v)])
            last_bc_v = p.emit("vector", lambda v, dst=dst, half=half: v.tensor_copy(
                dst[:, 512 * half:512 * (half + 1)], PB[0][:, :]), wait=[("T", tv)])
    bcV = last_bc_v

    g_xn_v = 0
    g_xn_s = 0
    g_ttr = 0
    g_ez1 = 0
    last_mmfree = {}
    scan_T_done = 0
    last_ac = last_Lev = last_vsc = last_se = 0
    last_sctr = last_keu = last_vU1 = last_lach = 0
    last_vbf = 0
    prev_store = [0, 0]
    ev_ev_hist = [0, 0]
    lx = 0

    for j in range(NB):
        r0 = j * 512
        lx = p.emit("sync", lambda s, r0=r0: s.dma_start(
            out=xblk[:].rearrange("p (a d) -> p a d", a=4),
            in_=xin[r0:r0 + 512, :].rearrange("(a p) d -> p a d", p=128)),
            wait=[("V", g_xn_v), ("S", g_xn_s)], dma="dx")
        xt = [xblk[:, 1024 * i:1024 * (i + 1)] for i in range(4)]
        guard = [("dx", lx), ("V", bcV), ("V", last_vsc), ("S", last_se)]
        emit_ln(p, xt, xt, stats, 0, escrA, guard)
        ln2s = emit_ln(p, xt, xt, stats, 24, escrA, [("S", p.val("S"))])
        g_xn_s = ln2s
        for i in range(4):
            p.emit("vector", lambda v, i=i: v.tensor_mul(xt[i], xt[i], gB[:]),
                   wait=[("S", ln2s), ("T", g_ttr)])
            p.emit("vector", lambda v, i=i: v.tensor_add(xnbf[:, 1024 * i:1024 * (i + 1)],
                                                         xt[i], bB[:]))
        xnv = p.val("V")
        g_xn_v = xnv
        if _stop(2, xblk):
            return
        for i in range(4):
            tv = p.emit("tensor", lambda t, i=i: [
                t.transpose(TB[i % 2][:, 128 * dg:128 * (dg + 1)],
                            xnbf[:, 1024 * i + 128 * dg:1024 * i + 128 * (dg + 1)],
                            identT[:]) for dg in range(8)][-1],
                wait=[("V", xnv), ("G", cstv), ("V", ev_ev_hist[i % 2])])
            ev_ev_hist[i % 2] = p.emit("vector", lambda v, i=i: v.tensor_copy(
                xnT[:].rearrange("p (g t) -> p g t", g=8)[:, :, 128 * i:128 * (i + 1)],
                TB[i % 2][:].rearrange("p (g t) -> p g t", g=8)),
                wait=[("T", tv)])
        g_ttr = p.val("T")
        xne = p.val("V")
        if _stop(3, xblk):
            return

        ez1 = [g_ez1, g_ez1]
        for dg in range(8):
            tv = p.emit("tensor", lambda t, dg=dg: [
                t.matmul(PB[dg % 2][:, :],
                         W["L0_attn_Wx"][:, 1024 * cg + 128 * dg:1024 * cg + 128 * (dg + 1)],
                         xnT[:, 512 * cg:512 * (cg + 1)],
                         start=(cg == 0), stop=(cg == 7)) for cg in range(8)][-1],
                wait=[("V", xne), ("dw", wldG), ("S", ez1[dg % 2]),
                      ("V", last_vU1 if dg < 2 else 0), ("S", last_ac if dg < 2 else 0)])
            ez1[dg % 2] = p.emit("scalar", lambda s, dg=dg: s.copy(
                z1T[:, 512 * dg:512 * (dg + 1)], PB[dg % 2][:, :]), wait=[("T", tv)])
        g_ez1 = p.val("S")
        z1all = g_ez1
        if _stop(4, xblk):
            return

        for i in range(4):
            def emit_proj(mat, banks, waits, lhs, i=i):
                tvs = []
                for f in range(2):
                    def mm(t, f=f, mat=mat, lhs=lhs, i=i, banks=banks):
                        ins = None
                        for cg in range(8):
                            ins = t.matmul(
                                banks[f][:, :],
                                lhs[:, 512 * cg + 128 * i:512 * cg + 128 * (i + 1)],
                                W[mat][:, 1024 * cg + 512 * f:1024 * cg + 512 * (f + 1)],
                                start=(cg == 0), stop=(cg == 7))
                        return ins
                    tvs.append(p.emit("tensor", mm, wait=waits[f]))
                return tvs

            pA = (PB[2], PB[3])
            pB = (PB[4], PB[5])

            wz = [[("S", z1all), last_mmfree.get((2 + f), ("V", 0))] for f in range(2)]
            tz = emit_proj("L0_attn_Ww", pA, wz, z1T)
            zadds = []
            for f in range(2):
                zadds.append(p.emit("vector", lambda v, f=f: v.tensor_add(
                    labuf[:, 512 * f:512 * (f + 1)], pA[f][:, :], bwB[:, 512 * f:512 * (f + 1)]),
                    wait=[("T", tz[f]), ("S", last_lach), ("T", scan_T_done)]))
            for f in range(2):
                last_mmfree[2 + f] = ("V", zadds[f])
            zv = zadds[-1]
            p.emit("scalar", lambda s: s.activation(out=escrA[:], in_=labuf[:], func=AF.Abs),
                   wait=[("V", zv), ("V", last_vsc)])
            p.emit("scalar", lambda s: s.activation(out=escrB[:], in_=escrA[:], func=AF.Exp,
                                                    scale=-1.0))
            sC = p.emit("scalar", lambda s: s.activation(out=escrA[:], in_=escrB[:], func=AF.Ln,
                                                         bias=1.0))
            last_lach = sC
            p.emit("vector", lambda v: v.tensor_scalar_max(escrC[:], labuf[:], 0.0),
                   wait=[("V", zv)])
            law = p.emit("vector", lambda v: v.scalar_tensor_tensor(
                labuf[:], escrA[:], -1.0, escrC[:], ALU.mult, ALU.subtract),
                wait=[("S", sC)])

            if _stop(5, labuf):
                return
            tv = p.emit("tensor", lambda t: [
                t.matmul(PB[0][:, 2 * dg:2 * (dg + 1)],
                         labuf[:, 128 * dg:128 * (dg + 1)],
                         cind[:], start=(dg == 0), stop=(dg == 7),
                         skip_group_check=True) for dg in range(8)][-1],
                wait=[("V", law), ("S", last_ac), ("V", last_vU1), ("G", cstv)])
            acs = p.emit("scalar", lambda s: s.activation(
                out=AC[:], in_=PB[0][:, 0:16], func=AF.Exp), wait=[("T", tv)])
            last_ac = acs
            tL = p.emit("tensor", lambda t: [
                t.matmul(PB[2 + f][:, :], U64[:],
                         labuf[:, 512 * f:512 * (f + 1)],
                         start=True, stop=True) for f in range(2)][-1],
                wait=[("V", law), ("V", last_Lev), ("S", acs)])
            vL = p.emit("vector", lambda v: [
                v.tensor_copy(Lsb[:, 512 * f:512 * (f + 1)], PB[2 + f][:, :])
                for f in range(2)][-1], wait=[("T", tL), ("S", last_se)])
            last_Lev = vL
            last_mmfree[2] = last_mmfree[3] = ("V", vL)
            tM = p.emit("tensor", lambda t: [
                t.matmul(PB[4 + f][:, :], Bsel[:], Lsb[:, 512 * f:512 * (f + 1)],
                         start=True, stop=True) for f in range(2)][-1],
                wait=[("V", vL), last_mmfree.get("B_all", ("V", 0))])
            vE3 = p.emit("vector", lambda v: [
                v.scalar_tensor_tensor(labuf[:, 512 * f:512 * (f + 1)], PB[4 + f][:, :],
                                       -2.0, Lsb[:, 512 * f:512 * (f + 1)],
                                       ALU.mult, ALU.add) for f in range(2)][-1],
                wait=[("T", tM)])
            se4 = p.emit("scalar", lambda s: s.activation(out=escrR[:], in_=Lsb[:],
                                                          func=AF.Exp),
                         wait=[("V", vE3), ("V", last_vsc)])
            vE1 = p.emit("vector", lambda v: [
                v.tensor_tensor(Lsb[:, 512 * f:512 * (f + 1)], Lsb[:, 512 * f:512 * (f + 1)],
                                PB[4 + f][:, :], ALU.subtract) for f in range(2)][-1],
                wait=[("S", se4)])
            last_mmfree[4] = last_mmfree[5] = ("V", vE1)
            p.emit("scalar", lambda s: s.activation(out=escrA[:], in_=Lsb[:], func=AF.Exp,
                                                    bias=-S0),
                   wait=[("V", vE1), ("V", last_vsc)])
            p.emit("scalar", lambda s: s.activation(out=escrB[:], in_=Lsb[:], func=AF.Exp,
                                                    scale=-1.0, bias=-S0))
            se = p.emit("scalar", lambda s: s.activation(out=escrC[:], in_=labuf[:],
                                                         func=AF.Exp, scale=-1.0))
            last_se = se

            if _stop(6, Lsb):
                return
            wk = [[("V", xne), last_mmfree.get(2 + f, ("V", 0))] for f in range(2)]
            tk = emit_proj("L0_attn_Wk", pA, wk, xnT)
            kds = []
            for f in range(2):
                fs = slice(512 * f, 512 * (f + 1))
                p.emit("vector", lambda v, f=f, fs=fs: v.tensor_mul(
                    Kdbf[:, fs], pA[f][:, :], escrB[:, fs]),
                    wait=[("T", tk[f]), ("S", se), ("T", last_sctr), ("T", last_keu)])
                p.emit("vector", lambda v, f=f, fs=fs: v.tensor_mul(
                    Kec0[0:64, fs], pA[f][0:64, :], escrC[0:64, fs]))
                kds.append(p.emit("vector", lambda v, f=f, fs=fs: v.tensor_mul(
                    Kec1[64:128, fs], pA[f][64:128, :], escrC[64:128, fs])))
            for f in range(2):
                last_mmfree[2 + f] = ("V", kds[f])
            wr = [[("V", xne), last_mmfree.get(4 + f, ("V", 0))] for f in range(2)]
            tr = emit_proj("L0_attn_Wr", pB, wr, xnT)
            ers = []
            for f in range(2):
                ers.append(p.emit("scalar", lambda s, f=f: s.activation(
                    out=labuf[:, 512 * f:512 * (f + 1)], in_=pB[f][:, :], func=AF.Exp,
                    scale=-1.0),
                    wait=[("T", tr[f]), ("V", vE3), ("S", se)]))
            for f in range(2):
                last_mmfree[4 + f] = ("S", ers[f])
            p.emit("vector", lambda v: v.tensor_scalar_add(labuf[:], labuf[:], 1.0),
                   wait=[("S", ers[-1])])
            p.emit("vector", lambda v: v.reciprocal(labuf[:], labuf[:]))
            p.emit("vector", lambda v: v.tensor_mul(Rqbf[:], labuf[:], escrA[:]))
            rxv = p.emit("vector", lambda v: v.tensor_mul(Rxbf[:], labuf[:], escrR[:]))
            wv = [[("V", xne), last_mmfree.get(2 + f, ("V", 0))] for f in range(2)]
            tvv = emit_proj("L0_attn_Wv", pA, wv, xnT)
            vbs = []
            for f in range(2):
                vbs.append(p.emit("scalar", lambda s, f=f: s.copy(
                    vbf[:, 512 * f:512 * (f + 1)], pA[f][:, :]),
                    wait=[("T", tvv[f]), ("T", scan_T_done)]))
            for f in range(2):
                last_mmfree[2 + f] = ("S", vbs[f])
            last_vbf = vbs[-1]
            vsc = rxv
            last_vsc = p.val("V")

            if _stop(7, escrR):
                return
            tev = {}
            for nm, (srcb, dstb) in enumerate(((Rqbf, RqT), (Kdbf, KdT), (Rxbf, RxT))):
                tv = p.emit("tensor", lambda t, srcb=srcb, nm=nm: [
                    t.transpose(TB[nm % 2][:, 128 * dg:128 * (dg + 1)],
                                srcb[:, 128 * dg:128 * (dg + 1)], identT[:])
                    for dg in range(8)][-1],
                    wait=[("V", last_vsc), ("V", tev.get(nm - 2, 0)),
                          ("V", ev_ev_hist[nm % 2])])
                tev[nm] = p.emit("vector", lambda v, dstb=dstb, nm=nm: v.tensor_copy(
                    dstb[:], TB[nm % 2][:]), wait=[("T", tv)])
                ev_ev_hist[nm % 2] = tev[nm]
            last_sctr = p.val("T")
            trv = p.val("V")

            if _stop(8, escrR):
                return
            vPm = [0, 0]
            tO_last = 0
            for h in range(16):
                g2, half = h // 2, h % 2
                prow = slice(64 * half, 64 * half + 64)
                tcol = slice(128 * g2, 128 * g2 + 128)
                hch = slice(64 * h, 64 * h + 64)
                ptb = PB[h % 2]
                tP = p.emit("tensor", lambda t, prow=prow, tcol=tcol, ptb=ptb: t.matmul(
                    ptb[:, 0:128], KdT[prow, tcol], RqT[prow, tcol],
                    start=True, stop=True, skip_group_check=True),
                    wait=[("V", trv), ("V", vPm[h % 2]), ("S", acs)])
                vPm[h % 2] = p.emit("vector", lambda v, ptb=ptb, h=h: v.tensor_tensor(
                    PTm[h % 2][:], ptb[:, 0:128], maskE[:], ALU.mult),
                    wait=[("T", tP), ("G", cstv)])

                def tO(t, h=h, hch=hch):
                    ob = PB[2 + h // 8]
                    osl = slice(64 * (h % 8), 64 * (h % 8) + 64)
                    return t.matmul(ob[:, osl], PTm[h % 2][:], vbf[:, hch],
                                    start=(h % 8 == 0), stop=False, skip_group_check=True)

                if PHASE <= 9:
                    continue
                tO_last = p.emit("tensor", tO,
                                 wait=[("V", vPm[h % 2]), ("S", last_vbf), ("V", vL)])

            def tO2(t):
                for g2 in range(8):
                    ob = PB[2 + g2 // 4]
                    fr = slice(128 * (g2 % 4), 128 * (g2 % 4) + 128)
                    t.matmul(ob[0:64, fr], RxT[:, 128 * g2:128 * g2 + 64],
                             Sblk[:, 128 * g2:128 * g2 + 128],
                             start=False, stop=False, skip_group_check=True)
                ins = None
                for h in range(16):
                    half = h % 2
                    hch = slice(64 * h, 64 * h + 64)
                    ins = t.matmul(
                        PB[4][64 * half:64 * half + 64, 64 * (h // 2):64 * (h // 2) + 64],
                        Kec0[:, hch], vbf[:, hch],
                        start=(h < 2), stop=(h >= 14),
                        tile_position=(0, 64 * half), skip_group_check=True)
                return ins
            if PHASE > 9:
                tO_last = p.emit("tensor", tO2,
                                 wait=[("V", last_vU1), ("S", last_vbf), ("V", kds[-1]),
                                       ("V", vE1), ("S", ers[-1])])

            if PHASE <= 9:
                _stop(0, escrR)
                return
            if _stop(10, S2):
                return
            def vU0(v):
                ins = None
                for g2 in range(8):
                    fs = slice(64 * g2, 64 * g2 + 64)
                    ins = v.scalar_tensor_tensor(S2[:, fs], S2[:, fs],
                                                 AC[:, 2 * g2:2 * g2 + 1],
                                                 PB[4][:, fs], ALU.mult, ALU.add)
                return ins
            p.emit("vector", vU0, wait=[("T", tO_last), ("S", acs)])
            p.emit("vector", lambda v: v.tensor_copy(
                Sblk[0:64, :].rearrange("p (g c) -> p g c", g=8)[:, :, 0:64],
                S2[0:64, :].rearrange("p (g c) -> p g c", g=8)))
            vU0v = p.emit("vector", lambda v: v.tensor_copy(
                Sblk[64:128, :].rearrange("p (g c) -> p g c", g=8)[:, :, 64:128],
                S2[64:128, :].rearrange("p (g c) -> p g c", g=8)))

            def tC1(t):
                for g2 in range(8):
                    ob = PB[2 + g2 // 4]
                    fr = slice(128 * (g2 % 4), 128 * (g2 % 4) + 128)
                    t.matmul(ob[64:128, fr], RxT[:, 128 * g2 + 64:128 * g2 + 128],
                             Sblk[:, 128 * g2:128 * g2 + 128],
                             start=False, stop=(g2 % 4 == 3), tile_position=(0, 64),
                             skip_group_check=True)
                ins = None
                for h in range(16):
                    half = h % 2
                    hch = slice(64 * h, 64 * h + 64)
                    ins = t.matmul(
                        PB[5][64 * half:64 * half + 64, 64 * (h // 2):64 * (h // 2) + 64],
                        Kec1[:, hch], vbf[:, hch],
                        start=(h < 2), stop=(h >= 14),
                        tile_position=(0, 64 * half), skip_group_check=True)
                return ins
            tC1v = p.emit("tensor", tC1, wait=[("V", vU0v)])
            last_keu = tC1v
            if _stop(10.8, S2):
                return

            def vU1(v):
                ins = None
                for g2 in range(8):
                    fs = slice(64 * g2, 64 * g2 + 64)
                    ins = v.scalar_tensor_tensor(S2[:, fs], S2[:, fs],
                                                 AC[:, 2 * g2 + 1:2 * g2 + 2],
                                                 PB[5][:, fs], ALU.mult, ALU.add)
                return ins
            p.emit("vector", vU1, wait=[("T", tC1v)])
            p.emit("vector", lambda v: v.tensor_copy(
                Sblk[0:64, :].rearrange("p (g c) -> p g c", g=8)[:, :, 0:64],
                S2[0:64, :].rearrange("p (g c) -> p g c", g=8)))
            vU1v = p.emit("vector", lambda v: v.tensor_copy(
                Sblk[64:128, :].rearrange("p (g c) -> p g c", g=8)[:, :, 64:128],
                S2[64:128, :].rearrange("p (g c) -> p g c", g=8)))
            last_vU1 = vU1v
            last_mmfree[4] = last_mmfree[5] = ("V", vU1v)
            last_mmfree["B_all"] = ("V", vU1v)

            if _stop(11, S2):
                return
            xr = p.emit("sync", lambda s, r0=r0, i=i: s.dma_start(
                out=xrel[:], in_=xin[r0 + 128 * i:r0 + 128 * (i + 1), :]),
                wait=[("V", kds[-1])], dma="dxr")

            vO = p.emit("vector", lambda v: [
                v.tensor_copy(outs[:, 512 * b:512 * (b + 1)], PB[2 + b][:, :])
                for b in range(2)][-1], wait=[("T", tC1v)])
            tT = p.emit("tensor", lambda t: [
                t.transpose(TB[0][:, 128 * dg:128 * (dg + 1)],
                            outs[:, 128 * dg:128 * (dg + 1)], identT[:])
                for dg in range(8)][-1],
                wait=[("V", vO), ("V", ev_ev_hist[0])])
            vT = p.emit("vector", lambda v: v.tensor_copy(outsT[:], TB[0][:]),
                        wait=[("T", tT)])
            ev_ev_hist[0] = vT
            ty = p.emit("tensor", lambda t: [
                t.matmul(PB[2 + f][:, :],
                         outsT[:, 128 * cg:128 * (cg + 1)],
                         W["L0_attn_Wo"][:, 1024 * cg + 512 * f:1024 * cg + 512 * (f + 1)],
                         start=(cg == 0), stop=(cg == 7))
                for f in range(2) for cg in range(8)][-1],
                wait=[("V", vT), ("V", vO)])
            xi = x1b[0]
            vy = p.emit("vector", lambda v, xi=xi: [
                v.tensor_tensor(xi[:, 512 * f:512 * (f + 1)],
                                xrel[:, 512 * f:512 * (f + 1)],
                                PB[2 + f][:, :], ALU.add) for f in range(2)][-1],
                wait=[("T", ty), ("dxr", xr), ("dso", prev_store[0])])
            last_mmfree[2] = last_mmfree[3] = ("V", vy)
            prev_store[0] = p.emit(
                "gpsimd", lambda g, xi=xi, r0=r0, i=i: g.dma_start(
                    out=x1o[r0 + 128 * i:r0 + 128 * (i + 1), :], in_=xi[:]),
                wait=[("V", vy)], dma="dso")
        scan_T_done = p.val("T")

    p.emit("gpsimd", lambda g: g.nop(), wait=[("dso", p.val("dso"))])



def build_stageB(p, es, T, x1i, x2o):
    nc = p.nc
    E, F = 16, 4096
    keys = nc.dram_tensor("L0_ffn_keys", [E, D], F32, kind="ExternalInput").ap()
    rw = nc.dram_tensor("L0_ffn_rw", [E, D], F32, kind="ExternalInput").ap()
    rb = nc.dram_tensor("L0_ffn_rb", [E], F32, kind="ExternalInput").ap()
    w1 = nc.dram_tensor("L0_ffn_w1", [E, F, D], F32, kind="ExternalInput").ap()
    b1 = nc.dram_tensor("L0_ffn_b1", [E, F], F32, kind="ExternalInput").ap()
    w2 = nc.dram_tensor("L0_ffn_w2", [E, D, F], F32, kind="ExternalInput").ap()
    b2 = nc.dram_tensor("L0_ffn_b2", [E, D], F32, kind="ExternalInput").ap()
    w1Tc = nc.dram_tensor("B_w1Tc", [16, 4, 128, 8192], BF16, kind="Internal").ap()
    w2Tc = nc.dram_tensor("B_w2Tc", [16, 4, 128, 8192], BF16, kind="Internal").ap()
    TH = T // 2
    NTT = TH // 128
    NBK = TH // 256
    sb = lambda name, shape, dt=F32: es.enter_context(nc.sbuf_tensor(name, shape, dt))

    identT = sb("B_identT", [128, 128], BF16)
    identF = sb("B_identF", [128, 128], F32)
    ones1 = sb("B_ones1", [1, 128], F32)
    onesbf = sb("B_onesbf", [1, 512], BF16)

    def c_memsets(g):
        g.memset(identT[:], 0.0)
        g.memset(identF[:], 0.0)
        g.memset(onesbf[:], 1.0)
        g.memset(onec[:], 1.0)
        return g.memset(ones1[:], 1.0)

    def c_fills(g):
        g.affine_select(out=identT[:], in_=identT[:], compare_op=ALU.not_equal,
                        fill=1.0, base=0, pattern=[[-1, 128]], channel_multiplier=1)
        return g.affine_select(out=identF[:], in_=identF[:], compare_op=ALU.not_equal,
                               fill=1.0, base=0, pattern=[[-1, 128]], channel_multiplier=1)

    cm = p.emit("gpsimd", c_memsets)
    cstv = p.emit("gpsimd", c_fills, wait=[("G", cm)])
    cstv = p.reg_floats(es, [EPS, -0.5], "B_cstf") or cstv

    PR = [es.enter_context(nc.psum_tensor(f"B_pr{i}", [128, 512], F32)) for i in range(2)]
    PH = [es.enter_context(nc.psum_tensor(f"B_ph{i}", [128, 512], F32)) for i in range(2)]
    PO = [es.enter_context(nc.psum_tensor(f"B_po{i}", [128, 512], F32)) for i in range(2)]
    TB = [es.enter_context(nc.psum_tensor(f"B_tb{i}", [128, 1024], BF16)) for i in range(2)]

    keysb = sb("B_keys", [16, 1024], F32)
    rwb = sb("B_rw", [16, 1024], F32)
    Cbf = sb("B_Cbf", [16, 1024], BF16)
    CT = sb("B_CT", [128, 8 * 16], BF16)
    b2sb = sb("B_b2", [16, 1024], BF16)
    ks2 = sb("B_ks2", [16, 1], F32)
    blog = sb("B_blog", [1, 16], F32)
    rbrow = sb("B_rbrow", [1, 16], F32)
    b1row = sb("B_b1row", [1, 1024], BF16)
    b1col = sb("B_b1col", [128, 8], F32)
    onec = sb("B_onec", [1, 1], BF16)

    l1 = p.emit("sync", lambda s: s.dma_start(out=keysb[:], in_=keys[:, :]), dma="drB")
    l2 = p.emit("sync", lambda s: s.dma_start(out=rwb[:], in_=rw[:, :]), dma="drB")
    l3 = p.emit("sync", lambda s: s.dma_start(out=rbrow[:], in_=rb[None, :]), dma="drB")
    l4 = p.emit("gpsimd", lambda g: g.dma_start(out=b2sb[:], in_=b2[:, :]), dma="dwB")
    v1 = p.emit("vector", lambda v: v.scalar_tensor_tensor(
        Cbf[:], keysb[:], 2.0, rwb[:], ALU.mult, ALU.add), wait=[("drB", l2)])
    p.emit("vector", lambda v: v.tensor_mul(rwb[:], keysb[:], keysb[:]),
           wait=[("drB", l2)])
    v2 = p.emit("vector", lambda v: v.reduce_sum(ks2[:], rwb[:], AX))
    tC = p.emit("tensor", lambda t: [
        t.transpose(TB[0][:, 16 * dg:16 * (dg + 1)],
                    Cbf[:, 128 * dg:128 * (dg + 1)], identT[0:16, 0:16])
        for dg in range(8)][-1], wait=[("V", v1), ("G", cstv)])
    vC = p.emit("vector", lambda v: v.tensor_copy(CT[:], TB[0][:, 0:128]),
                wait=[("T", tC)])
    tk2 = p.emit("tensor", lambda t: t.transpose(
        PH[0][0:1, 0:16], ks2[:], identF[0:16, 0:16]),
        wait=[("V", v2), ("G", cstv)])
    vk2 = p.emit("vector", lambda v: v.scalar_tensor_tensor(
        blog[:], PH[0][0:1, 0:16], -1.0, rbrow[:], ALU.mult, ALU.add),
        wait=[("T", tk2), ("drB", l3)])
    blogB = sb("B_blogB", [128, 16], F32)
    tbb = p.emit("tensor", lambda t: t.matmul(
        PH[1][:, 0:16], ones1[:], blog[:], start=True, stop=True),
        wait=[("V", vk2)])
    vbb = p.emit("vector", lambda v: v.tensor_copy(blogB[:], PH[1][:, 0:16]),
                 wait=[("T", tbb)])

    xblk = sb("B_xblk", [128, 2 * 1024])
    xn2bf = sb("B_xn2bf", [128, 4 * 1024], BF16)
    xn2T = sb("B_xn2T", [128, 8 * 2048], BF16)
    accb = sb("B_acc", [128, 16 * 1024])
    hidT = sb("B_hidT", [128, 8 * 2048], BF16)
    w1T = sb("B_w1T", [128, 8 * 1024], BF16)
    w2T = sb("B_w2T", [128, 8 * 1024], BF16)
    wstg = [sb(f"B_wstg{i}", [128, 1024], BF16) for i in range(4)]
    stats = sb("B_stats", [128, 32])
    gatew = sb("B_gatew", [128, 32 * 16])
    lg = sb("B_lg", [128, 16])
    l2b = sb("B_l2b", [128, 16])
    exb = sb("B_exb", [128, 16])
    cols = sb("B_cols", [128, 8])
    gTbf = sb("B_gTbf", [16, 128], BF16)

    ev_hist = [vC, 0]
    stg_T = [0, 0, 0, 0]
    ln_s_prev = 0
    last_xn_v = 0
    last_vai = 0
    last_sx = 0
    acc_free = 0
    tpos_guard = 0

    pp_store = {0: 0, 1: 0}
    for e in range(E):
        for fc in range(4):
            for wi, (wsrc, wcache) in enumerate(((w1, w1Tc), (w2, w2Tc))):
                wTbuf = w1T if wi == 0 else w2T
                for s8 in range(8):
                    sl4 = s8 % 4
                    if wi == 0:
                        ld = p.emit("gpsimd", lambda g, e=e, fc=fc, s8=s8, sl4=sl4:
                            g.dma_start(
                                out=wstg[sl4][:],
                                in_=w1[e, 1024 * fc + 128 * s8:1024 * fc + 128 * (s8 + 1), :]),
                            wait=[("T", stg_T[sl4])], dma=f"dw1{sl4}")
                    else:
                        ld = p.emit("gpsimd", lambda g, e=e, fc=fc, s8=s8, sl4=sl4:
                            g.dma_start(
                                out=wstg[sl4][:],
                                in_=w2[e, 128 * s8:128 * (s8 + 1), 1024 * fc:1024 * (fc + 1)]),
                            wait=[("T", stg_T[sl4])], dma=f"dw1{sl4}")
                    tv = p.emit("tensor", lambda t, s8=s8, sl4=sl4: [
                        t.transpose(TB[s8 % 2][:, 128 * dg:128 * (dg + 1)],
                                    wstg[sl4][:, 128 * dg:128 * (dg + 1)], identT[:])
                        for dg in range(8)][-1],
                        wait=[(f"dw1{sl4}", ld), ("V", ev_hist[s8 % 2])])
                    stg_T[sl4] = tv
                    ev_hist[s8 % 2] = p.emit("vector", lambda v, s8=s8, wTbuf=wTbuf:
                        v.tensor_copy(
                            wTbuf[:].rearrange("p (g f) -> p g f", g=8)
                            [:, :, 128 * s8:128 * (s8 + 1)],
                            TB[s8 % 2][:].rearrange("p (g f) -> p g f", g=8)),
                        wait=[("T", tv), ("dwc", pp_store[wi])])
                pp_store[wi] = p.emit("gpsimd", lambda g, e=e, fc=fc, wcache=wcache,
                                      wTbuf=wTbuf: g.dma_start(
                    out=wcache[e, fc], in_=wTbuf[:]),
                    wait=[("V", ev_hist[0]), ("V", ev_hist[1])], dma="dwc")
    pp_done = p.val("dwc")

    for th in range(2):
        t0g = th * TH
        half_T0 = p.val("T")
        half_S0 = p.val("S")
        for bk in range(NBK):
            r0 = t0g + bk * 256
            lx = p.emit("sync", lambda s, r0=r0: s.dma_start(
                out=xblk[:].rearrange("p (a d) -> p a d", a=2),
                in_=x1i[r0:r0 + 256, :].rearrange("(a p) d -> p a d", p=128)),
                wait=[("S", ln_s_prev), ("V", last_vai), ("V", acc_free)], dma="dxB")
            xt = [xblk[:, 1024 * i:1024 * (i + 1)] for i in range(2)]
            ln_s_prev = emit_ln(p, xt, xt, stats, 0, xn2bf[:, 0:1024],
                                [("dxB", lx), ("G", cstv), ("V", last_xn_v),
                                 ("T", tpos_guard)])
            c_s2, c_mu, c_rs = stats[:, 2:4], stats[:, 4:6], stats[:, 6:8]
            c_t, c_s2n = stats[:, 20:22], stats[:, 22:24]
            p.emit("vector", lambda v: v.tensor_mul(c_t[:], c_mu[:], c_mu[:]),
                   wait=[("S", ln_s_prev)])
            p.emit("vector", lambda v: v.scalar_tensor_tensor(
                c_t[:], c_t[:], -float(D), c_s2[:], ALU.mult, ALU.add))
            p.emit("vector", lambda v: v.tensor_mul(c_s2n[:], c_rs[:], c_rs[:]))
            vs2n = p.emit("vector", lambda v: v.tensor_mul(c_s2n[:], c_s2n[:], c_t[:]))
            for i in range(2):
                p.emit("vector", lambda v, i=i: v.tensor_copy(
                    xn2bf[:, 1024 * i:1024 * (i + 1)], xt[i]),
                    wait=[("S", ln_s_prev), ("T", tpos_guard), ("V", acc_free)])
            xnv = p.val("V")
            last_xn_v = xnv
            for i in range(2):
                ti_loc = bk * 2 + i
                ti_abs = th * NTT + ti_loc
                tv = p.emit("tensor", lambda t, i=i: [
                    t.transpose(TB[i % 2][:, 128 * dg:128 * (dg + 1)],
                                xn2bf[:, 1024 * i + 128 * dg:1024 * i + 128 * (dg + 1)],
                                identT[:]) for dg in range(8)][-1],
                    wait=[("V", xnv), ("V", ev_hist[i % 2]), ("G", cstv),
                          ("T", half_T0)])
                ev_hist[i % 2] = p.emit("vector", lambda v, i=i, ti_loc=ti_loc: v.tensor_copy(
                    xn2T[:].rearrange("p (g t) -> p g t", g=8)[:, :, 128 * ti_loc:128 * (ti_loc + 1)],
                    TB[i % 2][:].rearrange("p (g t) -> p g t", g=8)),
                    wait=[("T", tv)])
                trt = p.emit("tensor", lambda t, ti_loc=ti_loc: [
                    t.matmul(PR[0][:, 0:16],
                             xn2T[:, 2048 * cg + 128 * ti_loc:2048 * cg + 128 * (ti_loc + 1)],
                             CT[:, 16 * cg:16 * (cg + 1)],
                             start=(cg == 0), stop=(cg == 7)) for cg in range(8)][-1],
                    wait=[("V", ev_hist[i % 2]), ("V", vC), ("V", last_vai)])
                vlg = p.emit("vector", lambda v, i=i: v.scalar_tensor_tensor(
                    lg[:], PR[0][:, 0:16], c_s2n[:, i:i + 1], blogB[:],
                    ALU.subtract, ALU.add),
                    wait=[("T", trt), ("V", vs2n), ("S", last_sx), ("V", vbb)])
                p.emit("vector", lambda v: v.tensor_copy(l2b[:], lg[:]))
                p.emit("vector", lambda v: v.reduce_max(cols[:, 0:1], lg[:], AX))
                for r in range(3):
                    p.emit("vector", lambda v: v.reduce_max(cols[:, 1:2], l2b[:], AX))
                    p.emit("vector", lambda v: v.tensor_scalar(
                        exb[:], l2b[:], cols[:, 1:2], None, ALU.is_ge))
                    p.emit("vector", lambda v: v.scalar_tensor_tensor(
                        l2b[:], exb[:], -1e30, l2b[:], ALU.mult, ALU.add))
                p.emit("vector", lambda v: v.reduce_max(cols[:, 2:3], l2b[:], AX))
                vm1 = p.emit("vector", lambda v: v.tensor_scalar_mul(
                    cols[:, 3:4], cols[:, 0:1], -1.0))
                last_sx = p.emit("scalar", lambda s: s.activation(
                    out=exb[:], in_=lg[:], func=AF.Exp, bias=cols[:, 3:4]),
                    wait=[("V", vm1)])
                p.emit("vector", lambda v: v.tensor_scalar(
                    l2b[:], lg[:], cols[:, 2:3], None, ALU.is_ge), wait=[("S", last_sx)])
                p.emit("vector", lambda v: v.tensor_mul(exb[:], exb[:], l2b[:]))
                p.emit("vector", lambda v: v.reduce_sum(cols[:, 4:5], exb[:], AX))
                p.emit("vector", lambda v: v.reciprocal(cols[:, 5:6], cols[:, 4:5]))
                vgw = p.emit("vector", lambda v, ti_abs=ti_abs: v.tensor_scalar(
                    gatew[:, 16 * ti_abs:16 * (ti_abs + 1)], exb[:], cols[:, 5:6], None,
                    ALU.mult))
                tgT = p.emit("tensor", lambda t, ti_abs=ti_abs: t.transpose(
                    PH[0][0:16, 0:128], gatew[:, 16 * ti_abs:16 * (ti_abs + 1)],
                    identF[:]), wait=[("V", vgw), ("S", half_S0 if bk + i == 0 else 0),
                                      ("V", acc_free)])
                vgT = p.emit("vector", lambda v: v.tensor_copy(gTbf[:], PH[0][0:16, 0:128]),
                             wait=[("T", tgT)])
                tb2 = p.emit("tensor", lambda t: [
                    t.matmul((PR[1] if fh else PR[0])[:, :], gTbf[:],
                             b2sb[:, 512 * fh:512 * (fh + 1)], start=True, stop=True)
                    for fh in range(2)][-1],
                    wait=[("V", vgT), ("dwB", l4), ("V", last_vai)])
                lxr = p.emit("gpsimd", lambda g, ti_loc=ti_loc, ti_abs=ti_abs: g.dma_start(
                    out=accb[:, 1024 * ti_loc:1024 * (ti_loc + 1)],
                    in_=x1i[t0g + 128 * (ti_abs - th * NTT):t0g + 128 * (ti_abs - th * NTT) + 128, :]),
                    wait=[("V", acc_free), ("V", last_vai)], dma="dxB2")
                last_vai = p.emit("vector", lambda v, ti_loc=ti_loc: [
                    v.tensor_tensor(
                        accb[:, 1024 * ti_loc + 512 * fh:1024 * ti_loc + 512 * (fh + 1)],
                        accb[:, 1024 * ti_loc + 512 * fh:1024 * ti_loc + 512 * (fh + 1)],
                        (PR[1] if fh else PR[0])[:, :], ALU.add)
                    for fh in range(2)][-1],
                    wait=[("T", tb2), ("dxB2", lxr)])
            tpos_guard = p.val("T")

        hg = [0, 0]
        og = [0, 0]
        unit_T_guard = p.val("T")
        w1_used = p.val("T")
        for e in range(E):
            for fc in range(4):
                lb1 = p.emit("gpsimd", lambda g, e=e, fc=fc: g.dma_start(
                    out=b1row[:], in_=b1[e, 1024 * fc:1024 * (fc + 1)][None, :]),
                    wait=[("T", p.val("T"))], dma="dwB")
                tb1 = p.emit("tensor", lambda t: [
                    t.matmul(PH[0][:, s8:s8 + 1], b1row[:, 128 * s8:128 * (s8 + 1)],
                             onec[:], start=(s8 == 0), stop=(s8 == 7))
                    for s8 in range(8)][-1],
                    wait=[("dwB", lb1), ("S", hg[0]), ("S", hg[1])])
                vb1 = p.emit("vector", lambda v: v.tensor_copy(b1col[:], PH[0][:, 0:8]),
                             wait=[("T", tb1)])
                w1all = p.emit("sync", lambda s, e=e, fc=fc: s.dma_start(
                    out=w1T[:], in_=w1Tc[e, fc]),
                    wait=[("T", w1_used), ("dwc", pp_done)], dma="dwT1")
                for s8 in range(8):
                    for tt in range(TH // 512):
                        bsel = (s8 * 4 + tt) % 2
                        hb = PH[bsel]
                        tv = p.emit("tensor", lambda t, s8=s8, tt=tt, hb=hb: [
                            t.matmul(hb[:, :],
                                     w1T[:, 1024 * cg + 128 * s8:1024 * cg + 128 * (s8 + 1)],
                                     xn2T[:, 2048 * cg + 512 * tt:2048 * cg + 512 * (tt + 1)],
                                     start=(cg == 0), stop=(cg == 7)) for cg in range(8)][-1],
                            wait=[("dwT1", w1all), ("S", hg[bsel]), ("V", vb1)])
                        hg[bsel] = p.emit("scalar", lambda s, s8=s8, tt=tt, hb=hb: s.activation(
                            out=hidT[:, 2048 * s8 + 512 * tt:2048 * s8 + 512 * (tt + 1)],
                            in_=hb[:, :], func=AF.Gelu, bias=b1col[:, s8:s8 + 1]),
                            wait=[("T", tv), ("V", vb1),
                                  ("T", unit_T_guard if (s8, tt) == (0, 0) else 0)])
                hidall = p.val("S")
                w1_used = p.val("T")
                w2all = p.emit("sync", lambda s, e=e, fc=fc: s.dma_start(
                    out=w2T[:], in_=w2Tc[e, fc]),
                    wait=[("T", unit_T_guard)], dma="dwT2")
                for tt in range(NTT):
                    ti_abs = th * NTT + tt
                    for dh in range(2):
                        bsel = (tt * 2 + dh) % 2
                        ob = PO[bsel]
                        tv = p.emit("tensor", lambda t, tt=tt, dh=dh, ob=ob: [
                            t.matmul(ob[:, :],
                                     hidT[:, 2048 * fs + 128 * tt:2048 * fs + 128 * (tt + 1)],
                                     w2T[:, 1024 * fs + 512 * dh:1024 * fs + 512 * (dh + 1)],
                                     start=(fs == 0), stop=(fs == 7)) for fs in range(8)][-1],
                            wait=[("dwT2", w2all), ("S", hidall), ("V", og[bsel])])
                        og[bsel] = p.emit("vector", lambda v, tt=tt, dh=dh, ob=ob, e=e,
                                          ti_abs=ti_abs: v.scalar_tensor_tensor(
                            accb[:, 1024 * tt + 512 * dh:1024 * tt + 512 * (dh + 1)],
                            ob[:, :], gatew[:, 16 * ti_abs + e:16 * ti_abs + e + 1],
                            accb[:, 1024 * tt + 512 * dh:1024 * tt + 512 * (dh + 1)],
                            ALU.mult, ALU.add),
                            wait=[("T", tv), ("V", last_vai if e == 0 else 0)])
                unit_T_guard = p.val("T")
        stv = 0
        for tt in range(NTT):
            r0 = t0g + 128 * tt
            stv = p.emit("gpsimd", lambda g, tt=tt, r0=r0: g.dma_start(
                out=x2o[r0:r0 + 128, :], in_=accb[:, 1024 * tt:1024 * (tt + 1)]),
                wait=[("V", og[0]), ("V", og[1])], dma="dsoB")
        acc_free = p.emit("vector", lambda v: v.engine_nop(), wait=[("dsoB", stv)])
    p.emit("gpsimd", lambda g: g.nop(), wait=[("dsoB", p.val("dsoB"))])



def build_stageC(p, es, T, x2i, x3o):
    nc = p.nc
    NH, HD = 8, 128
    AEPS = 1e-6
    wts = {}
    wts["Wq"] = nc.dram_tensor("L1_attn_Wq", [D, D], F32, kind="ExternalInput").ap()
    wts["Wm"] = nc.dram_tensor("L1_attn_Wm", [D, D], F32, kind="ExternalInput").ap()
    wts["W1"] = nc.dram_tensor("L1_attn_W1", [D, 2 * D], F32, kind="ExternalInput").ap()
    wts["W2"] = nc.dram_tensor("L1_attn_W2", [2 * D, D], F32, kind="ExternalInput").ap()
    vecs = {k: nc.dram_tensor("L1_attn_" + k, [D], F32, kind="ExternalInput").ap()
            for k in ("g1", "b1", "g2", "b2")}
    NB = T // 256
    sb = lambda name, shape, dt=F32: es.enter_context(nc.sbuf_tensor(name, shape, dt))

    identT = sb("C_identT", [128, 128], BF16)
    ones1 = sb("C_ones1", [1, 128], F32)
    onescol = sb("C_onescol", [128, 1], BF16)

    def c_memsets(g):
        g.memset(identT[:], 0.0)
        g.memset(onescol[:], 1.0)
        return g.memset(ones1[:], 1.0)

    def c_fills(g):
        return g.affine_select(out=identT[:], in_=identT[:], compare_op=ALU.not_equal,
                               fill=1.0, base=0, pattern=[[-1, 128]], channel_multiplier=1)

    cm = p.emit("gpsimd", c_memsets)
    cstv = p.emit("gpsimd", c_fills, wait=[("G", cm)])
    cstv = p.reg_floats(es, [EPS, -0.5], "C_cstf") or cstv

    PBa = [es.enter_context(nc.psum_tensor(f"C_pa{i}", [128, 512], F32)) for i in range(2)]
    PBb = [es.enter_context(nc.psum_tensor(f"C_pb{i}", [128, 512], F32)) for i in range(2)]
    PKV = [es.enter_context(nc.psum_tensor(f"C_pkv{i}", [128, 512], F32)) for i in range(2)]
    TB = [es.enter_context(nc.psum_tensor(f"C_tb{i}", [128, 1024], BF16)) for i in range(2)]

    Wq = sb("C_Wq", [128, 8 * 1024], BF16)
    Wm = sb("C_Wm", [128, 8 * 1024], BF16)
    W1 = sb("C_W1", [128, 8 * 2048], BF16)
    W2 = sb("C_W2", [128, 16 * 1024], BF16)
    for k, buf, gdim in (("Wq", Wq, 8), ("Wm", Wm, 8), ("W1", W1, 8), ("W2", W2, 16)):
        p.emit("gpsimd", lambda g, k=k, buf=buf, gdim=gdim: g.dma_start(
            out=buf[:].rearrange("p (g d) -> p g d", g=gdim),
            in_=wts[k].rearrange("(g p) d -> p g d", p=128)), dma="dwC")
    wldC = p.val("dwC")
    row = sb("C_row", [1, 1024], F32)
    gbB = {k: sb(f"C_{k}B", [128, 1024]) for k in ("g1", "b1", "g2", "b2")}
    last_bc = 0
    for k in ("g1", "b1", "g2", "b2"):
        ld = p.emit("sync", lambda s, k=k: s.dma_start(out=row[:], in_=vecs[k][None, :]),
                    wait=[("V", last_bc)], dma="drC")
        for half in range(2):
            tv = p.emit("tensor", lambda t, half=half: t.matmul(
                PBa[0][:, :], ones1[:], row[:, 512 * half:512 * (half + 1)],
                start=True, stop=True),
                wait=[("drC", ld), ("G", cstv), ("V", last_bc)])
            last_bc = p.emit("vector", lambda v, k=k, half=half: v.tensor_copy(
                gbB[k][:, 512 * half:512 * (half + 1)], PBa[0][:, :]), wait=[("T", tv)])
    bcV = last_bc

    xblk = sb("C_xblk", [128, 2 * 1024])
    lnbf = sb("C_lnbf", [128, 2 * 1024], BF16)
    TshT = sb("C_TshT", [128, 4 * 512], BF16)
    featbf = sb("C_featbf", [128, 2 * 1024], BF16)
    vqbf = sb("C_vqbf", [128, 2 * 1024], BF16)
    scr1 = sb("C_scr1", [128, 1024])
    scr2 = sb("C_scr2", [128, 1024])
    scrF = sb("C_scrF", [128, 1024])
    msgsbf = sb("C_msgsbf", [128, 2 * 1024], BF16)
    msgobf = sb("C_msgobf", [128, 2 * 1024], BF16)
    h1T = sb("C_h1T", [128, 16 * 256], BF16)
    x3b = sb("C_x3b", [128, 2 * 1024])
    xrel = sb("C_xrel", [128, 1024])
    stats = sb("C_stats", [128, 16])
    KVbf = sb("C_KVbf", [128, 8 * 128], BF16)
    KsM = sb("C_KsM", [128, 8 * 8], BF16)
    Zb = sb("C_Zb", [128, 16])

    ev_hist = [bcV, 0]
    ln_s_prev = 0
    last_feat_v = 0
    last_xn_v = 0
    guardsC = {"ln2x": 0, "scr1": 0}

    def emit_block_lnq(bk, x_src, phase, consume):
        nonlocal ln_s_prev, last_xn_v
        r0 = bk * 256
        lx = p.emit("sync", lambda s, r0=r0: s.dma_start(
            out=xblk[:].rearrange("p (a d) -> p a d", a=2),
            in_=x_src[r0:r0 + 256, :].rearrange("(a p) d -> p a d", p=128)),
            wait=[("S", ln_s_prev), ("V", last_xn_v), ("V", guardsC["ln2x"])], dma="dxC")
        xt = [xblk[:, 1024 * i:1024 * (i + 1)] for i in range(2)]
        ln_s_prev = emit_ln(p, xt, xt, stats, 0, scr1,
                            [("dxC", lx), ("G", cstv), ("V", bcV), ("V", last_feat_v),
                             ("V", guardsC["scr1"]),
                             ("T", p.val("T") if (bk == 0 and phase == "B") else 0)])
        for i in range(2):
            p.emit("vector", lambda v, i=i: v.tensor_copy(
                lnbf[:, 1024 * i:1024 * (i + 1)], xt[i]),
                wait=[("S", ln_s_prev)])
        xnv = p.val("V")
        last_xn_v = xnv
        for i in range(2):
            tv = p.emit("tensor", lambda t, i=i: [
                t.transpose(TB[i % 2][:, 128 * dg:128 * (dg + 1)],
                            lnbf[:, 1024 * i + 128 * dg:1024 * i + 128 * (dg + 1)],
                            identT[:]) for dg in range(8)][-1],
                wait=[("V", xnv), ("V", ev_hist[i % 2]), ("G", cstv)])
            ev_hist[i % 2] = p.emit("vector", lambda v, i=i: v.tensor_copy(
                TshT[:].rearrange("p (g t) -> p g t", g=8)[:, :, 128 * i:128 * (i + 1)],
                TB[i % 2][:].rearrange("p (g t) -> p g t", g=8)),
                wait=[("T", tv)])
        lnTv = p.val("V")
        for i in range(2):
            for f in range(2):
                bank = (PBa if (phase == "A" or i % 2 == 0) else PBb)[f]
                key = (0, f) if phase == "A" else (i % 2, f)
                tv = p.emit("tensor", lambda t, i=i, f=f, bank=bank: [
                    t.matmul(bank[:, :],
                             TshT[:, 256 * cg + 128 * i:256 * cg + 128 * (i + 1)],
                             Wq[:, 1024 * cg + 512 * f:1024 * cg + 512 * (f + 1)],
                             start=(cg == 0), stop=(cg == 7)) for cg in range(8)][-1],
                    wait=[("V", lnTv), ("dwC", wldC)] + list(qfree[key]))
                qfree[key] = consume(i, f, bank, tv)

    qfree = {(a, b): [] for a in range(2) for b in range(2)}
    last_kv = 0
    def chainA(i, f, bank, tv):
        hf = slice(512 * f, 512 * (f + 1))
        sv = p.emit("scalar", lambda s, i=i, bank=bank, hf=hf: s.copy(
            vqbf[:, 1024 * i:1024 * (i + 1)][:, hf], bank[:, :]),
            wait=[("T", tv), ("T", last_kv)])
        p.emit("vector", lambda v, bank=bank, hf=hf: v.tensor_scalar_min(
            scr1[:, hf], bank[:, :], 0.0), wait=[("T", tv), ("S", p.val("S"))])
        se_ = p.emit("scalar", lambda s, hf=hf: s.activation(
            out=scr2[:, hf], in_=scr1[:, hf], func=AF.Exp),
            wait=[("V", p.val("V"))])
        fv2 = p.emit("vector", lambda v, bank=bank, hf=hf: v.tensor_scalar_max(
            scr1[:, hf], bank[:, :], 0.0), wait=[("S", se_)])
        fv = p.emit("vector", lambda v, i=i, hf=hf: v.tensor_add(
            featbf[:, 1024 * i:1024 * (i + 1)][:, hf], scr1[:, hf], scr2[:, hf]),
            wait=[("T", last_kv)])
        return [("V", fv2), ("S", sv)]

    for bk in range(NB):
        emit_block_lnq(bk, x2i, "A", chainA)
        featv = p.val("V")
        svv = p.val("S")
        def kvmm(t, bk=bk):
            ins = None
            for i in range(2):
                for h in range(8):
                    first = (bk == 0 and i == 0 and h % 4 == 0)
                    ins = t.matmul(
                        PKV[h // 4][:, 128 * (h % 4):128 * (h % 4) + 128],
                        featbf[:, 1024 * i + 128 * h:1024 * i + 128 * (h + 1)],
                        vqbf[:, 1024 * i + 128 * h:1024 * i + 128 * (h + 1)],
                        start=first, stop=(bk == NB - 1 and i == 1 and h % 4 == 3),
                        skip_group_check=True)
                    ins = t.matmul(
                        PBb[0][:, h:h + 1],
                        featbf[:, 1024 * i + 128 * h:1024 * i + 128 * (h + 1)],
                        onescol[:],
                        start=(bk == 0 and i == 0 and h == 0),
                        stop=(bk == NB - 1 and i == 1 and h == 7),
                        skip_group_check=True)
            return ins
        last_kv = p.emit("tensor", kvmm, wait=[("V", featv), ("S", svv)])
        last_feat_v = featv
    vkv = p.emit("vector", lambda v: [
        v.tensor_copy(KVbf[:, 512 * b:512 * (b + 1)], PKV[b][:, :]) for b in range(2)][-1],
        wait=[("T", last_kv)])
    p.emit("vector", lambda v: v.memset(KsM[:], 0.0))
    vks = p.emit("vector", lambda v: [
        v.tensor_copy(KsM[:, 8 * dg + dg:8 * dg + dg + 1], PBb[0][:, dg:dg + 1])
        for dg in range(8)][-1])
    kv_ready = p.val("V")

    qfree = {(a, b): [("V", kv_ready)] for a in range(2) for b in range(2)}
    st_prev = 0
    last_mo_s = 0
    last_vx3 = 0
    last_fTt = 0
    last_moT = 0
    def chainB(i, f, bank, tv):
        hf = slice(512 * f, 512 * (f + 1))
        p.emit("vector", lambda v, bank=bank, hf=hf: v.tensor_scalar_min(
            scr1[:, hf], bank[:, :], 0.0), wait=[("T", tv), ("S", p.val("S"))])
        se_ = p.emit("scalar", lambda s, hf=hf: s.activation(
            out=scr2[:, hf], in_=scr1[:, hf], func=AF.Exp),
            wait=[("V", p.val("V"))])
        fv2 = p.emit("vector", lambda v, bank=bank, hf=hf: v.tensor_scalar_max(
            scr1[:, hf], bank[:, :], 0.0), wait=[("S", se_)])
        p.emit("vector", lambda v, i=i, hf=hf: v.tensor_add(
            featbf[:, 1024 * i:1024 * (i + 1)][:, hf], scr1[:, hf], scr2[:, hf]),
            wait=[("T", last_fTt)])
        return [("V", fv2)]

    for bk in range(NB):
        r0 = bk * 256
        emit_block_lnq(bk, x2i, "B", chainB)
        featv = p.val("V")
        for i in range(2):
            tv = p.emit("tensor", lambda t, i=i: [
                t.transpose(TB[i % 2][:, 128 * dg:128 * (dg + 1)],
                            featbf[:, 1024 * i + 128 * dg:1024 * i + 128 * (dg + 1)],
                            identT[:]) for dg in range(8)][-1],
                wait=[("V", featv), ("V", ev_hist[i % 2])])
            last_fTt = tv
            ev_hist[i % 2] = p.emit("vector", lambda v, i=i: v.tensor_copy(
                TshT[:].rearrange("p (g t) -> p g t", g=8)[:, :, 128 * i:128 * (i + 1)],
                TB[i % 2][:].rearrange("p (g t) -> p g t", g=8)),
                wait=[("T", tv)])
        featTv = p.val("V")
        for i in range(2):
            tden = p.emit("tensor", lambda t, i=i: [
                t.matmul(PBa[i % 2][:, 0:8],
                         TshT[:, 256 * dg + 128 * i:256 * dg + 128 * (i + 1)],
                         KsM[:, 8 * dg:8 * (dg + 1)],
                         start=(dg == 0), stop=(dg == 7)) for dg in range(8)][-1],
                wait=[("V", featTv), ("V", kv_ready)] + list(qfree[(0, i)]))
            p.emit("vector", lambda v, i=i: v.tensor_scalar(
                Zb[:, 8 * i:8 * (i + 1)], PBa[i % 2][:, 0:8], AEPS, None, ALU.add),
                wait=[("T", tden)])
            zv = p.emit("vector", lambda v, i=i: v.reciprocal(
                Zb[:, 8 * i:8 * (i + 1)], Zb[:, 8 * i:8 * (i + 1)]))
            tmsg = p.emit("tensor", lambda t, i=i: [
                t.matmul(PBb[0 if h < 4 else 1][:, 128 * (h % 4):128 * (h % 4) + 128],
                         TshT[:, 256 * h + 128 * i:256 * h + 128 * (i + 1)],
                         KVbf[:, 128 * h:128 * (h + 1)],
                         start=(h % 4 == 0), stop=(h % 4 == 3),
                         skip_group_check=True) for h in range(8)][-1],
                wait=[("V", zv), ("V", kv_ready), ("S", last_mo_s)]
                     + list(qfree[(1, 0)]) + list(qfree[(1, 1)]))
            for h in range(8):
                sv = p.emit("scalar", lambda s, i=i, h=h: s.activation(
                    out=msgsbf[:, 1024 * i + 128 * h:1024 * i + 128 * (h + 1)],
                    in_=PBb[0 if h < 4 else 1][:, 128 * (h % 4):128 * (h % 4) + 128],
                    func=AF.Identity, scale=Zb[:, 8 * i + h:8 * i + h + 1]),
                    wait=[("T", tmsg)])
            last_mo_s = p.val("S")
        msgv = p.val("S")
        for i in range(2):
            tv = p.emit("tensor", lambda t, i=i: [
                t.transpose(TB[i % 2][:, 128 * dg:128 * (dg + 1)],
                            msgsbf[:, 1024 * i + 128 * dg:1024 * i + 128 * (dg + 1)],
                            identT[:]) for dg in range(8)][-1],
                wait=[("S", msgv), ("V", ev_hist[i % 2])])
            ev_hist[i % 2] = p.emit("vector", lambda v, i=i: v.tensor_copy(
                TshT[:].rearrange("p (g t) -> p g t", g=8)[:, :, 128 * i:128 * (i + 1)],
                TB[i % 2][:].rearrange("p (g t) -> p g t", g=8)),
                wait=[("T", tv)])
        msgsTv = p.val("V")
        for i in range(2):
            tmo = p.emit("tensor", lambda t, i=i: [
                t.matmul((PBa if i % 2 == 0 else PBb)[f][:, :],
                         TshT[:, 256 * cg + 128 * i:256 * cg + 128 * (i + 1)],
                         Wm[:, 1024 * cg + 512 * f:1024 * cg + 512 * (f + 1)],
                         start=(cg == 0), stop=(cg == 7))
                for f in range(2) for cg in range(8)][-1],
                wait=[("V", msgsTv), ("S", last_mo_s)])
            vmo = p.emit("vector", lambda v, i=i: [
                v.tensor_copy(scrF[:, 512 * f:512 * (f + 1)],
                              (PBa if i % 2 == 0 else PBb)[f][:, :])
                for f in range(2)][-1], wait=[("T", tmo), ("S", p.val("S"))])
            sF = emit_ln(p, [scrF[:]], [scr1[:]], stats, 10, scr2, [("V", vmo)])
            xr = p.emit("sync", lambda s, r0=r0, i=i: s.dma_start(
                out=xrel[:], in_=x2i[r0 + 128 * i:r0 + 128 * (i + 1), :]),
                wait=[("V", last_vx3)], dma="dx2C")
            p.emit("vector", lambda v, i=i: v.tensor_mul(scr1[:], scr1[:], gbB["g1"][:]),
                   wait=[("S", sF)])
            p.emit("vector", lambda v: v.tensor_add(scr1[:], scr1[:], gbB["b1"][:]))
            guardsC["ln2x"] = p.emit("vector", lambda v, i=i: v.tensor_add(
                scr1[:], scr1[:], xblk[:, 1024 * i:1024 * (i + 1)]))
            vx3 = p.emit("vector", lambda v, i=i: v.tensor_add(
                x3b[:, 1024 * i:1024 * (i + 1)], scr1[:], xrel[:]),
                wait=[("dx2C", xr), ("dsoC", st_prev)])
            last_vx3 = vx3
            p.emit("vector", lambda v, i=i: v.tensor_copy(
                msgobf[:, 1024 * i:1024 * (i + 1)], scrF[:]),
                wait=[("T", last_moT)])
        movt = p.val("V")
        for i in range(2):
            tv = p.emit("tensor", lambda t, i=i: [
                t.transpose(TB[i % 2][:, 128 * dg:128 * (dg + 1)],
                            msgobf[:, 1024 * i + 128 * dg:1024 * i + 128 * (dg + 1)],
                            identT[:]) for dg in range(8)][-1],
                wait=[("V", movt), ("V", ev_hist[i % 2])])
            last_moT = tv
            ev_hist[i % 2] = p.emit("vector", lambda v, i=i: v.tensor_copy(
                TshT[:].rearrange("p (g t) -> p g t", g=8)[:, :, 128 * i:128 * (i + 1)],
                TB[i % 2][:].rearrange("p (g t) -> p g t", g=8)),
                wait=[("T", tv)])
        moTv = p.val("V")
        hfree = [0, 0]
        for fs in range(16):
            hb = PBa[fs % 2]
            tv = p.emit("tensor", lambda t, fs=fs, hb=hb: [
                t.matmul(hb[:, 0:256],
                         W1[:, 2048 * cg + 128 * fs:2048 * cg + 128 * (fs + 1)],
                         TshT[:, 256 * cg:256 * (cg + 1)],
                         start=(cg == 0), stop=(cg == 7)) for cg in range(8)][-1],
                wait=[("V", moTv), ("S", hfree[fs % 2]), ("S", last_mo_s if fs < 2 else 0)])
            hfree[fs % 2] = p.emit("scalar", lambda s, fs=fs, hb=hb: s.activation(
                out=h1T[:, 256 * fs:256 * (fs + 1)], in_=hb[:, 0:256], func=AF.Relu),
                wait=[("T", tv)])
        h1v = p.val("S")
        for i in range(2):
            tml = p.emit("tensor", lambda t, i=i: [
                t.matmul((PBb if i % 2 == 0 else PBa)[dh][:, :],
                         h1T[:, 256 * fs + 128 * i:256 * fs + 128 * (i + 1)],
                         W2[:, 1024 * fs + 512 * dh:1024 * fs + 512 * (dh + 1)],
                         start=(fs == 0), stop=(fs == 15))
                for dh in range(2) for fs in range(16)][-1],
                wait=[("S", h1v), ("S", last_mo_s)])
            vml = p.emit("vector", lambda v, i=i: [
                v.tensor_copy(scrF[:, 512 * dh:512 * (dh + 1)],
                              (PBb if i % 2 == 0 else PBa)[dh][:, :])
                for dh in range(2)][-1], wait=[("T", tml)])
            sF = emit_ln(p, [scrF[:]], [scr1[:]], stats, 10, scr2, [("V", vml)])
            p.emit("vector", lambda v: v.tensor_mul(scr1[:], scr1[:], gbB["g2"][:]),
                   wait=[("S", sF)])
            p.emit("vector", lambda v: v.tensor_add(scr1[:], scr1[:], gbB["b2"][:]))
            vx3 = p.emit("vector", lambda v, i=i: v.tensor_add(
                x3b[:, 1024 * i:1024 * (i + 1)], x3b[:, 1024 * i:1024 * (i + 1)], scr1[:]))
            guardsC["scr1"] = vx3
            st_prev = p.emit("gpsimd", lambda g, r0=r0, i=i: g.dma_start(
                out=x3o[r0 + 128 * i:r0 + 128 * (i + 1), :],
                in_=x3b[:, 1024 * i:1024 * (i + 1)]),
                wait=[("V", vx3)], dma="dsoC")
        last_mo_s = p.val("S")
        for kk in qfree:
            qfree[kk] = [("V", p.val("V")), ("S", p.val("S"))]
    p.emit("gpsimd", lambda g: g.nop(), wait=[("dsoC", p.val("dsoC"))])



FP4_THR = [0.25, 0.75, 1.25, 1.75, 2.5, 3.5, 5.0]


def AP0(ap, k):
    from concourse.ap import AP as _AP
    dims = [list(d) for d in ap.ap] + [[0, k]]
    return _AP(tensor=ap.tensor, offset=ap.offset, ap=dims)


def _apslice(ap, cs):
    return ap[:, cs]


def build_stageD(p, es, T, x3i, x4o):
    nc = p.nc
    I = 2816
    wg = nc.dram_tensor("L1_ffn_w_gate", [I, D], F32, kind="ExternalInput").ap()
    wu = nc.dram_tensor("L1_ffn_w_up", [I, D], F32, kind="ExternalInput").ap()
    wd = nc.dram_tensor("L1_ffn_w_down", [D, I], F32, kind="ExternalInput").ap()
    NB = T // 256
    sb = lambda name, shape, dt=F32: es.enter_context(nc.sbuf_tensor(name, shape, dt))

    identT = sb("D_identT", [128, 128], BF16)

    def c_memsets(g):
        return g.memset(identT[:], 0.0)

    def c_fills(g):
        return g.affine_select(out=identT[:], in_=identT[:], compare_op=ALU.not_equal,
                               fill=1.0, base=0, pattern=[[-1, 128]], channel_multiplier=1)

    cm = p.emit("gpsimd", c_memsets)
    cstv = p.emit("gpsimd", c_fills, wait=[("G", cm)])
    cstv = p.reg_floats(es, [EPS, -0.5], "D_cstf") or cstv

    PA = [es.enter_context(nc.psum_tensor(f"D_pa{i}", [128, 512], F32)) for i in range(2)]
    PBp = [es.enter_context(nc.psum_tensor(f"D_pb{i}", [128, 512], F32)) for i in range(2)]
    PD = [es.enter_context(nc.psum_tensor(f"D_pd{i}", [128, 512], F32)) for i in range(2)]
    TB = [es.enter_context(nc.psum_tensor(f"D_tb{i}", [128, 1024], BF16)) for i in range(2)]

    wgT = sb("D_wgT", [128, 8 * I], BF16)
    wuT = sb("D_wuT", [128, 8 * I], BF16)
    wdT = sb("D_wdT", [128, 22 * 1024], BF16)
    wstgD = sb("D_wstg", [128, I])
    wqbf = sb("D_wqbf", [128, I], BF16)
    scrA = sb("D_scrA", [128, I])
    scrQ = sb("D_scrQ", [128, I])
    scrM = sb("D_scrM", [128, 2 * (I // 16)])
    scrS = sb("D_scrS", [128, I], BF16)
    xblk = sb("D_xblk", [128, 1024])
    xqbf = sb("D_xqbf", [128, 2 * 1024], BF16)
    xqT = sb("D_xqT", [128, 8 * 256], BF16)
    hbuf = [wstgD, sb("D_hbuf1", [128, I], BF16)]
    hqbf = wqbf
    hqT = sb("D_hqT", [128, 22 * 128], BF16)
    xrel = sb("D_xrel", [128, 1024])
    sil = sb("D_sil", [128, 512])
    x4t = xrel
    stats = sb("D_stats", [128, 16])

    gq = {"v": 0, "s": 0, "g": 0}

    def emit_qdq(src_ap, sgn_src, out_ap, W, waits, out_guard=0, pre=None):
        NBK16 = W // 16
        BV = NBK16
        if pre is None:
            sv_abs = p.emit("scalar", lambda s: s.activation(
                out=scrA[:, 0:W], in_=src_ap, func=AF.Abs),
                wait=list(waits) + [("V", gq["v"]), ("G", gq["g"])])
            ss = p.emit("scalar", lambda s: s.sign(out=scrS[:, 0:W], in_=sgn_src))
        else:
            sv_abs, ss = pre
        def _half(eng, b0, b1):
            nb = b1 - b0
            cs = slice(16 * b0, 16 * b1)
            sA = scrA[:, cs]
            sQ = scrQ[:, cs]
            sS = scrS[:, cs]
            a3 = lambda ap: ap.rearrange("p (b k) -> p b k", k=16)
            W2 = 16 * nb
            o1, o2, o3 = 16 * b0, 16 * b0 + W2 // 2, 16 * b0 + W2 // 2 + W2 // 4
            o4 = o3 + W2 // 8
            m0 = slice(b0, b1)
            m1 = slice(NBK16 + b0, NBK16 + b1)
            E = lambda fn, w=(): p.emit(eng, fn, wait=w)
            p.emit("vector", lambda e: e.tensor_tensor(
                scrQ[:, o1:o1 + W2 // 2].rearrange("p (b k) -> p b k", k=8),
                a3(sA)[:, :, 0:8], a3(sA)[:, :, 8:16], ALU.max),
                wait=[("S", sv_abs)])
            p.emit("vector", lambda e: e.tensor_tensor(
                scrQ[:, o2:o2 + W2 // 4].rearrange("p (b k) -> p b k", k=4),
                scrQ[:, o1:o1 + W2 // 2].rearrange("p (b k) -> p b k", k=8)[:, :, 0:4],
                scrQ[:, o1:o1 + W2 // 2].rearrange("p (b k) -> p b k", k=8)[:, :, 4:8],
                ALU.max))
            p.emit("vector", lambda e: e.tensor_tensor(
                scrQ[:, o3:o3 + W2 // 8].rearrange("p (b k) -> p b k", k=2),
                scrQ[:, o2:o2 + W2 // 4].rearrange("p (b k) -> p b k", k=4)[:, :, 0:2],
                scrQ[:, o2:o2 + W2 // 4].rearrange("p (b k) -> p b k", k=4)[:, :, 2:4],
                ALU.max))
            p.emit("vector", lambda e: e.tensor_tensor(
                scrM[:, m0],
                scrQ[:, o3:o3 + W2 // 8].rearrange("p (b k) -> p b k", k=2)[:, :, 0:1]
                .rearrange("p b k -> p (b k)"),
                scrQ[:, o3:o3 + W2 // 8].rearrange("p (b k) -> p b k", k=2)[:, :, 1:2]
                .rearrange("p b k -> p (b k)"), ALU.max))
            p.emit("vector", lambda e: e.tensor_scalar(
                scrM[:, m0], scrM[:, m0], 1.0 / 6.0, 1e-12, ALU.mult, ALU.max))
            rv = p.emit("vector", lambda e: e.reciprocal(scrM[:, m1], scrM[:, m0]))
            E(lambda e: e.tensor_tensor(a3(sA), a3(sA), AP0(scrM[:, m1], 16), ALU.mult),
              w=[("V", rv)])
            if eng == "vector":
                E(lambda e: e.tensor_scalar(sQ[:], sA[:], FP4_THR[0], None, ALU.is_gt))
                for c in FP4_THR[1:4]:
                    E(lambda e, c=c: e.scalar_tensor_tensor(sQ[:], sA[:], c, sQ[:],
                                                            ALU.is_gt, ALU.add))
                E(lambda e: e.tensor_scalar_mul(sQ[:], sQ[:], 0.5))
                for c in FP4_THR[4:6]:
                    E(lambda e, c=c: e.scalar_tensor_tensor(sQ[:], sA[:], c, sQ[:],
                                                            ALU.is_gt, ALU.add))
                for _ in range(2):
                    E(lambda e: e.scalar_tensor_tensor(sQ[:], sA[:], FP4_THR[6], sQ[:],
                                                       ALU.is_gt, ALU.add))
            else:
                tmp = _apslice(out_ap, cs)
                E(lambda e: e.tensor_scalar(sQ[:], sA[:], FP4_THR[0], None, ALU.is_gt),
                  w=[("T", out_guard)])
                for c in FP4_THR[1:4]:
                    E(lambda e, c=c: e.tensor_scalar(tmp, sA[:], c, None, ALU.is_gt))
                    E(lambda e: e.tensor_tensor(sQ[:], sQ[:], tmp, ALU.add))
                E(lambda e: e.tensor_scalar_mul(sQ[:], sQ[:], 0.5))
                for c in FP4_THR[4:6]:
                    E(lambda e, c=c: e.tensor_scalar(tmp, sA[:], c, None, ALU.is_gt))
                    E(lambda e: e.tensor_tensor(sQ[:], sQ[:], tmp, ALU.add))
                E(lambda e: e.tensor_scalar(tmp, sA[:], FP4_THR[6], None, ALU.is_gt))
                E(lambda e: e.tensor_tensor(sQ[:], sQ[:], tmp, ALU.add))
                E(lambda e: e.tensor_tensor(sQ[:], sQ[:], tmp, ALU.add))
            E(lambda e: e.tensor_tensor(a3(sQ), a3(sQ), AP0(scrM[:, m0], 16), ALU.mult))
            return E(lambda e, cs=cs: e.tensor_mul(
                _apslice(out_ap, cs), scrQ[:, cs], sS[:]),
                w=[("S", ss), ("T", out_guard)])

        fvs = [_half("vector", 0, BV)]
        fvs.append(_half("gpsimd", BV, NBK16) if BV < NBK16 else 0)
        gq["v"] = fvs[0]
        gq["g"] = fvs[1]
        gq["s"] = ss
        return fvs[0], fvs[1]

    wld_prev = 0
    ev_hist = [("V", 0), ("V", 0)]
    for wsrc, wdst in ((wg, wgT), (wu, wuT)):
        for ch in range(22):
            ld = p.emit("sync", lambda s, wsrc=wsrc, ch=ch: s.dma_start(
                out=wstgD[:, 0:1024], in_=wsrc[128 * ch:128 * (ch + 1), :]),
                wait=[("S", gq["s"]), ("V", gq["v"]), ("G", gq["g"]),
                      ("T", wld_prev)], dma="dwD")
            fv, fg = emit_qdq(wstgD[:, 0:1024], wstgD[:, 0:1024], wqbf[:, 0:1024], 1024,
                          [("dwD", ld), ("G", cstv)], out_guard=wld_prev)
            for tb in range(8):
                tv = p.emit("tensor", lambda t, tb=tb: t.transpose(
                    TB[tb % 2][:, 128 * (tb // 2):128 * (tb // 2) + 128],
                    wqbf[:, 128 * tb:128 * (tb + 1)], identT[:]),
                    wait=[("V", fv), ("G", fg), ev_hist[tb % 2], ("G", cstv)])
                wld_prev = tv
                ev_hist[tb % 2] = ("V", p.emit("vector", lambda v, tb=tb, ch=ch, wdst=wdst: v.tensor_copy(
                    wdst[:, I * tb + 128 * ch:I * tb + 128 * (ch + 1)],
                    TB[tb % 2][:, 128 * (tb // 2):128 * (tb // 2) + 128]),
                    wait=[("T", tv)]))
    for ch in range(8):
        ld = p.emit("sync", lambda s, ch=ch: s.dma_start(
            out=wstgD[:], in_=wd[128 * ch:128 * (ch + 1), :]),
            wait=[("S", gq["s"]), ("V", gq["v"]), ("T", wld_prev)], dma="dwD")
        fv, fg = emit_qdq(wstgD[:], wstgD[:], wqbf[:], I, [("dwD", ld), ("G", cstv)],
                      out_guard=wld_prev)
        for tb in range(22):
            tv = p.emit("tensor", lambda t, tb=tb: t.transpose(
                TB[tb % 2][:, 128 * ((tb // 2) % 4):128 * ((tb // 2) % 4) + 128],
                wqbf[:, 128 * tb:128 * (tb + 1)], identT[:]),
                wait=[("V", fv), ("G", fg), ev_hist[tb % 2], ("G", cstv)])
            wld_prev = tv
            ev_hist[tb % 2] = ("V", p.emit("vector", lambda v, tb=tb, ch=ch: v.tensor_copy(
                wdT[:, 1024 * tb + 128 * ch:1024 * tb + 128 * (ch + 1)],
                TB[tb % 2][:, 128 * ((tb // 2) % 4):128 * ((tb // 2) % 4) + 128]),
                wait=[("T", tv)]))
    wq_done_v = p.val("V")
    wq_done_t = p.val("T")

    ln_s_prev = 0
    last_xq_t = 0
    last_hq_t = wq_done_t
    last_x4 = 0
    st_prev = 0
    gu_free = [0, 0]
    dn_free = [0, 0]
    fr_ch = [(0, 512), (512, 1024), (1024, 1536), (1536, 2048), (2048, 2560), (2560, I)]
    for bk in range(NB):
        r0 = bk * 256
        for i in range(2):
            lx = p.emit("sync", lambda s, r0=r0, i=i: s.dma_start(
                out=xblk[:], in_=x3i[r0 + 128 * i:r0 + 128 * (i + 1), :]),
                wait=[("S", ln_s_prev), ("S", gq["s"]), ("V", gq["v"]),
                      ("G", gq["g"])], dma="dxD")
            ln_s_prev = emit_ln(p, [xblk[:]], [xblk[:]], stats, 0, scrA[:, 0:1024],
                                [("dxD", lx), ("G", cstv), ("V", gq["v"]),
                                 ("G", gq["g"]),
                                 ("V", wq_done_v if bk == 0 else 0)])
            fv, fg = emit_qdq(xblk[:], xblk[:], xqbf[:, 1024 * i:1024 * (i + 1)], 1024,
                          [("S", ln_s_prev)], out_guard=last_xq_t)
            tvs = p.emit("tensor", lambda t, i=i: [
                t.transpose(TB[i % 2][:, 128 * dg:128 * (dg + 1)],
                            xqbf[:, 1024 * i + 128 * dg:1024 * i + 128 * (dg + 1)],
                            identT[:]) for dg in range(8)][-1],
                wait=[("V", fv), ("G", fg), ev_hist[i % 2],
                      ("T", wq_done_t if bk == 0 and i == 0 else 0)])
            last_xq_t = tvs
            ev_hist[i % 2] = ("V", p.emit("vector", lambda v, i=i: v.tensor_copy(
                xqT[:].rearrange("p (g t) -> p g t", g=8)[:, :, 128 * i:128 * (i + 1)],
                TB[i % 2][:].rearrange("p (g t) -> p g t", g=8)),
                wait=[("T", tvs)]))
        xqTv = p.val("V")
        hv = [0, 0]
        qres = [None, None]
        for i in range(2):
            hb_i = hbuf[i]
            for ci, (n0, n1) in enumerate(fr_ch):
                nw = n1 - n0
                tg = p.emit("tensor", lambda t, i=i, n0=n0, nw=nw, ci=ci: [
                    t.matmul(PA[ci % 2][:, 0:nw],
                             xqT[:, 256 * cg + 128 * i:256 * cg + 128 * (i + 1)],
                             wgT[:, I * cg + n0:I * cg + n0 + nw],
                             start=(cg == 0), stop=(cg == 7)) for cg in range(8)][-1],
                    wait=[("V", xqTv), ("V", gu_free[ci % 2])])
                tu = p.emit("tensor", lambda t, i=i, n0=n0, nw=nw, ci=ci: [
                    t.matmul(PBp[ci % 2][:, 0:nw],
                             xqT[:, 256 * cg + 128 * i:256 * cg + 128 * (i + 1)],
                             wuT[:, I * cg + n0:I * cg + n0 + nw],
                             start=(cg == 0), stop=(cg == 7)) for cg in range(8)][-1])
                sg = p.emit("scalar", lambda s, n0=n0, nw=nw, ci=ci: s.activation(
                    out=sil[:, 0:nw], in_=PA[ci % 2][:, 0:nw], func=AF.Silu),
                    wait=[("T", tg), ("V", gu_free[(ci + 1) % 2])])
                gu_free[ci % 2] = p.emit("vector", lambda v, n0=n0, nw=nw, ci=ci,
                                         hb_i=hb_i: v.tensor_tensor(
                    hb_i[:, n0:n0 + nw], sil[:, 0:nw],
                    PBp[ci % 2][:, 0:nw], ALU.mult),
                    wait=[("T", tu), ("S", sg), ("S", gq["s"])])
            hv[i] = p.val("V")
            if i == 0:
                pa0 = p.emit("scalar", lambda s: s.activation(
                    out=scrA[:, 0:I], in_=hbuf[0][:], func=AF.Abs),
                    wait=[("V", hv[0]), ("V", gq["v"]), ("G", gq["g"]),
                          ("S", gq["s"])])
                ps0 = p.emit("scalar", lambda s: s.sign(out=scrS[:, 0:I],
                                                        in_=hbuf[0][:]))
                qres[0] = (pa0, ps0)
        for i in range(2):
            fv, fg = emit_qdq(
                hbuf[i][:], hbuf[i][:], hqbf[:], I, [("V", hv[1])],
                out_guard=last_hq_t, pre=qres[0] if i == 0 else None)
            tl = 0
            for tb in range(22):
                tv = p.emit("tensor", lambda t, tb=tb: t.transpose(
                    TB[tb % 2][:, 128 * ((tb // 2) % 4):128 * ((tb // 2) % 4) + 128],
                    hqbf[:, 128 * tb:128 * (tb + 1)], identT[:]),
                    wait=[("V", fv), ("G", fg), ev_hist[tb % 2]])
                tl = tv
                last_hq_t = tv
                ev_hist[tb % 2] = ("S", p.emit("scalar", lambda s, tb=tb: s.copy(
                    hqT[:, 128 * tb:128 * (tb + 1)],
                    TB[tb % 2][:, 128 * ((tb // 2) % 4):128 * ((tb // 2) % 4) + 128]),
                    wait=[("T", tv)]))
            hqTv = p.val("S")
            xr = p.emit("sync", lambda s, r0=r0, i=i: s.dma_start(
                out=xrel[:], in_=x3i[r0 + 128 * i:r0 + 128 * (i + 1), :]),
                wait=[("V", last_x4), ("dsoD", st_prev)], dma="dx2D")
            for dh in range(2):
                td = p.emit("tensor", lambda t, dh=dh: [
                    t.matmul(PD[dh][:, :],
                             hqT[:, 128 * ng:128 * (ng + 1)],
                             wdT[:, 1024 * ng + 512 * dh:1024 * ng + 512 * (dh + 1)],
                             start=(ng == 0), stop=(ng == 21)) for ng in range(22)][-1],
                    wait=[("S", hqTv), ("V", dn_free[dh])])
                dn_free[dh] = p.emit("vector", lambda v, dh=dh: v.tensor_tensor(
                    x4t[:, 512 * dh:512 * (dh + 1)], xrel[:, 512 * dh:512 * (dh + 1)],
                    PD[dh][:, :], ALU.add),
                    wait=[("T", td), ("dx2D", xr), ("dsoD", st_prev)])
            last_x4 = p.val("V")
            st_prev = p.emit("gpsimd", lambda g, r0=r0, i=i: g.dma_start(
                out=x4o[r0 + 128 * i:r0 + 128 * (i + 1), :], in_=x4t[:]),
                wait=[("V", last_x4)], dma="dsoD")
    p.emit("gpsimd", lambda g: g.nop(), wait=[("dsoD", p.val("dsoD"))])



def _build_A(T):
    nc = bass.Bass("TRN2", target_bir_lowering=False)
    es = ExitStack()
    p = P(nc)
    xin = nc.dram_tensor("x_in", [T, D], F32, kind="ExternalInput").ap()
    x1o = nc.dram_tensor("x1", [T, D], F32, kind="ExternalOutput").ap()
    with es:
        build_stageA(p, es, T, xin, x1o)
        p.build("stageA")
    return nc


def _build_B(T):
    nc = bass.Bass("TRN2", target_bir_lowering=False)
    es = ExitStack()
    p = P(nc)
    x1i = nc.dram_tensor("x1_in", [T, D], F32, kind="ExternalInput").ap()
    x2o = nc.dram_tensor("x2", [T, D], F32, kind="ExternalOutput").ap()
    with es:
        build_stageB(p, es, T, x1i, x2o)
        p.build("stageB")
    return nc


def _build_C(T):
    nc = bass.Bass("TRN2", target_bir_lowering=False)
    es = ExitStack()
    p = P(nc)
    x2i = nc.dram_tensor("x2_in", [T, D], F32, kind="ExternalInput").ap()
    x3o = nc.dram_tensor("x3", [T, D], F32, kind="ExternalOutput").ap()
    with es:
        build_stageC(p, es, T, x2i, x3o)
        p.build("stageC")
    return nc


def _build_D(T):
    nc = bass.Bass("TRN2", target_bir_lowering=False)
    es = ExitStack()
    p = P(nc)
    x3i = nc.dram_tensor("x3_in", [T, D], F32, kind="ExternalInput").ap()
    x4o = nc.dram_tensor("x4", [T, D], F32, kind="ExternalOutput").ap()
    with es:
        build_stageD(p, es, T, x3i, x4o)
        p.build("stageD")
    return nc


A_KEYS = ["L0_attn_Wx", "L0_attn_Ww", "L0_attn_Wk", "L0_attn_Wv", "L0_attn_Wr",
          "L0_attn_Wo", "L0_attn_ln_g", "L0_attn_ln_b", "L0_attn_bw"]
B_KEYS = ["L0_ffn_keys", "L0_ffn_rw", "L0_ffn_rb", "L0_ffn_w1", "L0_ffn_b1",
          "L0_ffn_w2", "L0_ffn_b2"]
C_KEYS = ["L1_attn_Wq", "L1_attn_Wm", "L1_attn_W1", "L1_attn_W2",
          "L1_attn_g1", "L1_attn_b1", "L1_attn_g2", "L1_attn_b2"]
D_KEYS = ["L1_ffn_w_gate", "L1_ffn_w_up", "L1_ffn_w_down"]


def _build_fused(T):
    nc = bass.Bass("TRN2", target_bir_lowering=False)
    p = P(nc)
    ges = ExitStack()
    p.const_es = ges
    p.reg_floats(ges, [EPS, -0.5, -S0], "cstfG")
    xin = nc.dram_tensor("x_in", [T, D], F32, kind="ExternalInput").ap()
    x1 = nc.dram_tensor("x1_s", [T, D], F32, kind="Internal").ap()
    x2 = nc.dram_tensor("x2_s", [T, D], F32, kind="Internal").ap()
    x3 = nc.dram_tensor("x3_s", [T, D], F32, kind="Internal").ap()
    x4o = nc.dram_tensor("x4", [T, D], F32, kind="ExternalOutput").ap()
    with ExitStack() as esA:
        build_stageA(p, esA, T, xin, x1)
    p.barrier_all()
    with ExitStack() as esB:
        build_stageB(p, esB, T, x1, x2)
    p.barrier_all()
    with ExitStack() as esC:
        build_stageC(p, esC, T, x2, x3)
    p.barrier_all()
    with ExitStack() as esD:
        build_stageD(p, esD, T, x3, x4o)
    p.build("fused")
    ges.close()
    return nc


ALL_KEYS = A_KEYS + B_KEYS + C_KEYS + D_KEYS


def kernel(**inputs):
    T = inputs["L0_attn_x"].shape[1]
    nc = _build_fused(T)
    in_maps = []
    for c in range(8):
        m = {k: np.ascontiguousarray(inputs[k]) for k in ALL_KEYS}
        m["x_in"] = np.ascontiguousarray(inputs["L0_attn_x"][c])
        in_maps.append(m)
    res = run_bass_kernel_spmd(nc, in_maps, core_ids=list(range(8)))
    return np.stack([r["x4"] for r in res.results], axis=0)


def kernel_unfused(**inputs):
    T = inputs["L0_attn_x"].shape[1]

    def run(nc, keys, xkey, xs):
        in_maps = []
        for c in range(8):
            m = {k: np.ascontiguousarray(inputs[k]) for k in keys}
            m[xkey] = np.ascontiguousarray(xs[c])
            in_maps.append(m)
        return run_bass_kernel_spmd(nc, in_maps, core_ids=list(range(8)))

    xs = [inputs["L0_attn_x"][c] for c in range(8)]
    r = run(_build_A(T), A_KEYS, "x_in", xs)
    xs = [q["x1"] for q in r.results]
    r = run(_build_B(T), B_KEYS, "x1_in", xs)
    xs = [q["x2"] for q in r.results]
    r = run(_build_C(T), C_KEYS, "x2_in", xs)
    xs = [q["x3"] for q in r.results]
    r = run(_build_D(T), D_KEYS, "x3_in", xs)
    return np.stack([q["x4"] for q in r.results], axis=0)
